# version 04
# baseline (speedup 1.0000x reference)
import numpy as np
from contextlib import ExitStack

import concourse.bass as bass
import concourse.mybir as mybir
from concourse.bass_utils import run_bass_kernel_spmd

F32 = mybir.dt.float32
BF16 = mybir.dt.bfloat16
I32 = mybir.dt.int32
AF = mybir.ActivationFunctionType
OP = mybir.AluOpType
AX = mybir.AxisListType

D = 1024
T = 2048
S = 2048
H = 16
E = 64
NT = 16
QB = 512
NBLK = 4
INNER = 4096
NK = 8
LN_EPS1 = 1e-5
LN_EPS2 = 1e-12

ENGS = ("sync", "gpsimd", "tensor", "vector", "scalar")


class Em:
    def __init__(self):
        self.ops = {e: [] for e in ENGS}
        self.cnt = {s: 0 for s in ("dS", "dG", "cT", "cV", "cS")}
        self.h = {}

    def on(self, eng, fn, inc=None, inc_by=1):
        tok = None
        if inc is not None:
            self.cnt[inc] += inc_by
            tok = (inc, self.cnt[inc])

            def wrapped(E, fn=fn, semn=inc, inc_by=inc_by):
                fn(E).then_inc(self.h[semn], inc_by)
            self.ops[eng].append(wrapped)
        else:
            self.ops[eng].append(fn)
        return tok

    def dma(self, eng, out, in_, slow=False):
        semn = "dG" if eng == "gpsimd" else "dS"

        def fn(E, out=out, in_=in_, slow=slow):
            if slow:
                with E.bass.allow_non_contiguous_dma("small const load"):
                    return E.dma_start(out, in_)
            return E.dma_start(out, in_)
        return self.on(eng, fn, inc=semn, inc_by=16)

    def dmat(self, out, in_):
        return self.on("sync", lambda E, out=out, in_=in_: E.dma_start_transpose(out, in_),
                       inc="dS", inc_by=16)

    def wait(self, eng, tok):
        if tok is None:
            return
        semn, v = tok
        self.ops[eng].append(lambda E, semn=semn, v=v: E.wait_ge(self.h[semn], v))

    def waits(self, eng, toks):
        best = {}
        for t in toks:
            if t is None:
                continue
            s, v = t
            best[s] = max(best.get(s, 0), v)
        for s, v in best.items():
            self.wait(eng, (s, v))

    def vt(self, fn):
        tok = self.on("vector", fn, inc="cV")
        self.wait("vector", tok)
        return tok

    def barrier(self):
        toks = [(s, c) for s, c in self.cnt.items() if c > 0]
        for e in ENGS:
            self.waits(e, toks)


def build(debug=False):
    nc = bass.Bass("TRN2", target_bir_lowering=False, detect_race_conditions=False)

    x_d = nc.dram_tensor("x", [T, D], F32, kind="ExternalInput").ap()
    src_d = nc.dram_tensor("src", [S, D], F32, kind="ExternalInput").ap()
    val_d = nc.dram_tensor("val", [S, D], F32, kind="ExternalInput").ap()
    wq_d = nc.dram_tensor("wq", [D, D], F32, kind="ExternalInput").ap()
    wk_d = nc.dram_tensor("wk", [D, D], F32, kind="ExternalInput").ap()
    wv_d = nc.dram_tensor("wv", [D, D], F32, kind="ExternalInput").ap()
    wo_d = nc.dram_tensor("wo", [D, D], F32, kind="ExternalInput").ap()
    bq_d = nc.dram_tensor("bq", [D], F32, kind="ExternalInput").ap()
    bk_d = nc.dram_tensor("bk", [D], F32, kind="ExternalInput").ap()
    bv_d = nc.dram_tensor("bv", [D], F32, kind="ExternalInput").ap()
    bo_d = nc.dram_tensor("bo", [D], F32, kind="ExternalInput").ap()
    lng_d = nc.dram_tensor("lng", [D], F32, kind="ExternalInput").ap()
    lnb_d = nc.dram_tensor("lnb", [D], F32, kind="ExternalInput").ap()
    wg_d = nc.dram_tensor("wg", [INNER, D], F32, kind="ExternalInput").ap()
    wu_d = nc.dram_tensor("wu", [INNER, D], F32, kind="ExternalInput").ap()
    wd_d = nc.dram_tensor("wd", [D, INNER], F32, kind="ExternalInput").ap()
    cb_d = nc.dram_tensor("cb", [D], F32, kind="ExternalInput").ap()
    wmod_d = nc.dram_tensor("wmod", [D, 6 * D], F32, kind="ExternalInput").ap()
    y_d = nc.dram_tensor("y", [T, D], F32, kind="ExternalOutput").ap()

    KIND = "ExternalOutput" if debug else "Internal"
    scr_src = nc.dram_tensor("scr_src", [S, D], BF16, kind=KIND).ap()
    scr_val = nc.dram_tensor("scr_val", [S, D], BF16, kind=KIND).ap()
    scr_ln1 = nc.dram_tensor("scr_ln1", [T, D], BF16, kind=KIND).ap()
    scr_qn = nc.dram_tensor("scr_qn", [T, D], BF16, kind=KIND).ap()
    scr_kn = nc.dram_tensor("scr_kn", [S, D], BF16, kind=KIND).ap()
    scr_xq = nc.dram_tensor("scr_xq", [T, D], BF16, kind=KIND).ap()
    scr_guq = nc.dram_tensor("scr_guq", [T, INNER], BF16, kind=KIND).ap()
    scr_xu = nc.dram_tensor("scr_xu", [T, D], F32, kind=KIND).ap()
    scr_wg = nc.dram_tensor("scr_wg", [INNER, D], BF16, kind=KIND).ap()
    scr_wu = nc.dram_tensor("scr_wu", [INNER, D], BF16, kind=KIND).ap()
    scr_wd = nc.dram_tensor("scr_wd", [D, INNER], BF16, kind=KIND).ap()
    scr_rows = nc.dram_tensor("scr_rows", [2, D], F32).ap()

    em = Em()
    es = ExitStack()
    for s in ("dS", "dG", "cT", "cV", "cS"):
        em.h[s] = es.enter_context(nc.semaphore(s))

    def sb(stk, name, shape, dt):
        return stk.enter_context(nc.sbuf_tensor(name, shape, dt))

    def ps(stk, name, shape):
        return stk.enter_context(nc.psum_tensor(name, shape, F32))

    ones_f = sb(es, "ones_f", [1, 128], F32)
    ones_c = sb(es, "ones_c", [128, 1], F32)
    ones_b = sb(es, "ones_b", [1, 128], BF16)
    c_cols = sb(es, "c_cols", [128, 8], F32)
    c_bf = sb(es, "c_bf", [128, 8], BF16)
    bq_bf = sb(es, "bq_bf", [1, D], BF16)
    bk_bf = sb(es, "bk_bf", [1, D], BF16)
    bv_bf = sb(es, "bv_bf", [1, D], BF16)
    bo_bf = sb(es, "bo_bf", [1, D], BF16)
    gb_b = sb(es, "gb_b", [128, D], F32)
    al1c = sb(es, "al1c", [128, NT], F32)
    be1c = sb(es, "be1c", [128, NT], F32)
    rs1c = sb(es, "rs1c", [128, NT], F32)
    s1nc = sb(es, "s1nc", [128, NT], F32)
    rs2c = sb(es, "rs2c", [128, NT], F32)
    mwg_c = sb(es, "mwg_c", [128, 1], F32)
    mwu_c = sb(es, "mwu_c", [128, 1], F32)
    mwd_c = sb(es, "mwd_c", [128, 1], F32)
    tiny = sb(es, "tiny", [128, 64], F32)
    tiny2 = sb(es, "tiny2", [128, 32], F32)
    trow = sb(es, "trow", [1, 16], F32)

    mid = ExitStack()
    g2_b = sb(mid, "g2_b", [128, D], F32)
    b2_b = sb(mid, "b2_b", [128, D], F32)
    cb1_b = sb(mid, "cb1_b", [128, D], F32)
    sb_b = sb(mid, "sb_b", [128, D], F32)
    ca1_b = sb(mid, "ca1_b", [128, D], F32)
    sa_b = sb(mid, "sa_b", [128, D], F32)
    ca1_col = sb(mid, "ca1_col", [128, 8], F32)
    sa_col = sb(mid, "sa_col", [128, 8], F32)

    attn = ExitStack()
    knT = sb(attn, "knT", [128, NK * S], BF16)
    v_aug = sb(attn, "v_aug", [128, NT * H * 65], BF16)

    x_src = x_d.rearrange("(t p) d -> t p d", p=128)

    t_c = em.dma("sync", c_cols[:, :], cb_d.rearrange("(k p) -> p k", p=128), slow=True)
    em.on("vector", lambda E: E.memset(ones_f[:, :], 1.0))
    em.on("vector", lambda E: E.memset(ones_c[:, :], 1.0))
    t_ones = em.on("vector", lambda E: E.memset(ones_b[:, :], 1.0), inc="cV")
    t_bias = []
    for buf, bsrc in ((bq_bf, bq_d), (bk_bf, bk_d), (bv_bf, bv_d), (bo_bf, bo_d)):
        t_bias.append(em.dma("gpsimd", buf[:, :], bsrc[None, :]))
    em.wait("scalar", t_c)
    for _f in (AF.Silu, AF.Relu, AF.Exp, AF.Square, AF.Ln, AF.Abs, AF.Identity):
        em.on("scalar", lambda E, _f=_f: E.activation(tiny2[:, 28:29], c_cols[:, 0:1], _f,
                                                      **({"scale": 0.0, "bias": 1.0} if _f == AF.Ln else {})))
    t_cbf = em.on("scalar", lambda E: E.activation(c_bf[:, :], c_cols[:, :], AF.Silu), inc="cS")

    with ExitStack() as p0:
        lng_row = sb(p0, "lng_row", [1, D], F32)
        lnb_row = sb(p0, "lnb_row", [1, D], F32)
        mod_row = sb(p0, "mod_row", [1, 6 * D], F32)
        drv = sb(p0, "drv", [1, 2 * D], F32)
        tmp_row = sb(p0, "tmp_row", [1, 2 * D], F32)
        wmod_buf = [sb(p0, f"wmod{i}", [128, 6 * D], BF16) for i in range(2)]
        psum_mod = [ps(p0, f"psmod{i}", [1, 512]) for i in range(6)]
        psum_bc = ps(p0, "psbc", [128, 512])

        t_lng = em.dma("sync", lng_row[:, :], lng_d[None, :])
        t_lnb = em.dma("sync", lnb_row[:, :], lnb_d[None, :])

        wmod_src = wmod_d.rearrange("(c p) n -> c p n", p=128)
        em.wait("tensor", t_cbf)
        em.wait("tensor", t_ones)
        ev_toks = []
        wmod_war = [None, None]
        for p2 in range(2):
            for kc in range(NK):
                buf = wmod_buf[kc % 2]
                em.wait("gpsimd", wmod_war[kc % 2])
                tl = em.dma("gpsimd", buf[:, :], wmod_src[kc])
                em.wait("tensor", tl)
                for ncn in range(6):
                    col0 = (p2 * 6 + ncn) * 512
                    tok = em.on("tensor", lambda E, kc=kc, ncn=ncn, col0=col0, buf=buf: E.matmul(
                        psum_mod[ncn][0:1, :], c_bf[:, kc:kc + 1], buf[:, col0:col0 + 512],
                        start=(kc == 0), stop=(kc == NK - 1)), inc="cT")
                wmod_war[kc % 2] = tok
            em.wait("vector", tok)
            for ncn in range(6):
                col0 = (p2 * 6 + ncn) * 512
                tv = em.on("vector", lambda E, ncn=ncn, col0=col0: E.tensor_copy(
                    mod_row[0:1, col0:col0 + 512], psum_mod[ncn][0:1, :]), inc="cV")
            ev_toks.append(tv)
            if p2 == 0:
                em.wait("tensor", tv)

        def rsl(t_, i):
            return t_[0:1, i * D:(i + 1) * D]

        sa_row, ca_row, ga_row = rsl(mod_row, 0), rsl(mod_row, 1), rsl(mod_row, 2)
        sbr_row, cb_row, gbm_row = rsl(mod_row, 3), rsl(mod_row, 4), rsl(mod_row, 5)
        ga_c, gb_c = rsl(drv, 0), rsl(drv, 1)
        t1r, t2r = rsl(tmp_row, 0), rsl(tmp_row, 1)

        em.waits("vector", ev_toks + [t_lng, t_lnb])
        for g_in, g_out in ((ga_row, ga_c), (gbm_row, gb_c)):
            em.vt(lambda E, g_in=g_in: E.tensor_scalar(t1r, g_in, 0.0, None, OP.is_ge))
            em.vt(lambda E: E.tensor_scalar(t1r, t1r, 2.0, -1.0, OP.mult, OP.add))
            em.vt(lambda E, g_in=g_in: E.tensor_scalar(t2r, g_in, -1.0, None, OP.mult))
            em.vt(lambda E, g_in=g_in: E.tensor_tensor(t2r, g_in, t2r, OP.max))
            em.vt(lambda E: E.tensor_scalar(t2r, t2r, 0.25, None, OP.max))
            em.vt(lambda E, g_out=g_out: E.tensor_tensor(g_out, t1r, t2r, OP.mult))
        g2_row, b2_row = ga_row, gbm_row
        em.vt(lambda E: E.tensor_tensor(g2_row, ga_c, lng_row[0:1, :], OP.mult))
        em.vt(lambda E: E.tensor_tensor(b2_row, ga_c, lnb_row[0:1, :], OP.mult))
        em.vt(lambda E: E.tensor_scalar(ca_row, ca_row, 1.0, None, OP.add))
        t_rows = em.vt(lambda E: E.tensor_scalar(cb_row, cb_row, 1.0, None, OP.add))

        em.wait("sync", t_rows)
        t_w1 = em.dma("sync", scr_rows[0:1, :], sa_row)
        t_w2 = em.dma("sync", scr_rows[1:2, :], ca_row)
        em.waits("sync", [t_w1, t_w2])
        t_sac = em.dma("sync", sa_col[:, :], scr_rows[0:1, :].rearrange("o (k p) -> (o p) k", p=128), slow=True)
        t_cac = em.dma("sync", ca1_col[:, :], scr_rows[1:2, :].rearrange("o (k p) -> (o p) k", p=128), slow=True)

        em.wait("tensor", t_rows)
        prev_ev = None
        for row, tile in ((g2_row, g2_b), (b2_row, b2_b), (cb_row, cb1_b), (sbr_row, sb_b),
                          (gb_c, gb_b), (ca_row, ca1_b), (sa_row, sa_b)):
            for half in range(2):
                em.wait("tensor", prev_ev)
                tmm = em.on("tensor", lambda E, row=row, half=half: E.matmul(
                    psum_bc[:, :], ones_f[0:1, :], row[0:1, half * 512:(half + 1) * 512],
                    start=True, stop=True), inc="cT")
                em.wait("vector", tmm)
                prev_ev = em.on("vector", lambda E, tile=tile, half=half: E.tensor_copy(
                    tile[:, half * 512:(half + 1) * 512], psum_bc[:, :]), inc="cV")

    em.barrier()

    t_src = em.dma("gpsimd", scr_src[:, :], src_d[:, :])
    t_val = em.dma("gpsimd", scr_val[:, :], val_d[:, :])
    t_vinit = em.on("vector", lambda E: E.memset(v_aug[:, :], 1.0), inc="cV")

    kY = 0

    with ExitStack() as pk:
        sT_sb = sb(pk, "sT_sb", [128, NK * S], BF16)
        w_sb = sb(pk, "w_sb", [128, NK * D], BF16)
        ek_t = sb(pk, "ek_t", [128, D], F32)
        sq_t = sb(pk, "sq_t", [128, D], F32)
        r_t = sb(pk, "r_t", [128, D], F32)
        kn_bf = [sb(pk, f"kn_bf{i}", [128, D], BF16) for i in range(2)]
        psum_kv = [ps(pk, f"pskv{i}", [128, D]) for i in range(2)]

        kn_dma_toks = []
        t_vaug_last = None
        for which in ("k", "v"):
            w_in = wk_d if which == "k" else wv_d
            scr_in = scr_src if which == "k" else scr_val
            t_stage = t_src if which == "k" else t_val
            bias_row = bk_bf if which == "k" else bv_bf
            w_rows = w_in.rearrange("(c p) n -> c p n", p=128)
            t_w = [em.dma("gpsimd", w_sb[:, ic * D:(ic + 1) * D], w_rows[ic]) for ic in range(NK)]
            em.wait("sync", t_stage)
            t_T = [em.dmat(sT_sb[:, ic * S:(ic + 1) * S], scr_in[:, ic * 128:(ic + 1) * 128])
                   for ic in range(NK)]
            if which == "v":
                em.waits("gpsimd", [t_last_mm])
                em.waits("sync", [t_last_mm])
            em.waits("tensor", t_w + t_T + [t_ones] + t_bias)
            psum_free = [None, None]
            for st in range(NT):
                pid = st % 2
                em.wait("tensor", psum_free[pid])
                for ic in range(NK):
                    stat = sT_sb[:, ic * S + st * 128: ic * S + st * 128 + 128]
                    for oc in range(2):
                        em.on("tensor", lambda E, stat=stat, ic=ic, oc=oc, pid=pid: E.matmul(
                            psum_kv[pid][:, oc * 512:(oc + 1) * 512], stat,
                            w_sb[:, ic * D + oc * 512: ic * D + (oc + 1) * 512],
                            start=(ic == 0), stop=False))
                for oc in range(2):
                    t_mm = em.on("tensor", lambda E, oc=oc, pid=pid, bias_row=bias_row: E.matmul(
                        psum_kv[pid][:, oc * 512:(oc + 1) * 512], ones_b[0:1, :],
                        bias_row[0:1, oc * 512:(oc + 1) * 512], start=False, stop=True), inc="cT")
                t_last_mm = t_mm
                if which == "k":
                    em.wait("scalar", t_mm)
                    em.on("scalar", lambda E, pid=pid: E.activation(r_t[:, :], psum_kv[pid][:, :], AF.Relu, scale=-1.0))
                    t_exp = em.on("scalar", lambda E: E.activation(r_t[:, :], r_t[:, :], AF.Exp, scale=-1.0), inc="cS")
                    em.wait("vector", t_exp)
                    t_ek = em.on("vector", lambda E, pid=pid: E.scalar_tensor_tensor(
                        ek_t[:, :], r_t[:, :], -1.0, psum_kv[pid][:, :], OP.add, OP.max), inc="cV")
                    psum_free[pid] = t_ek
                    em.wait("scalar", t_ek)
                    t_sq = em.on("scalar", lambda E: E.activation(sq_t[:, :], ek_t[:, :], AF.Square), inc="cS")
                    em.wait("vector", t_sq)
                    t_red = em.on("vector", lambda E: E.tensor_reduce(
                        tiny[:, kY:kY + H], sq_t[:, :].rearrange("p (h e) -> p h e", h=H), AX.X, OP.add), inc="cV")
                    em.wait("vector", t_red)
                    t_nrm = em.on("vector", lambda E: E.tensor_copy(tiny2[:, 0:H], tiny[:, kY:kY + H]), inc="cV")
                    em.wait("scalar", t_nrm)
                    em.on("scalar", lambda E: E.activation(tiny2[:, 0:H], tiny2[:, 0:H], AF.Ln))
                    t_rn = em.on("scalar", lambda E: E.activation(tiny2[:, 0:H], tiny2[:, 0:H], AF.Exp, scale=-0.5), inc="cS")
                    em.wait("vector", t_rn)
                    buf = kn_bf[st % 2]
                    if len(kn_dma_toks) >= 2:
                        em.wait("vector", kn_dma_toks[-2])
                    t_knb = em.on("vector", lambda E, buf=buf: E.tensor_tensor(
                        buf[:, :].rearrange("p (h e) -> p h e", h=H),
                        ek_t[:, :].rearrange("p (h e) -> p h e", h=H),
                        tiny2[:, 0:H, None].broadcast_to([128, H, 64]), OP.mult), inc="cV")
                    em.wait("sync", t_knb)
                    tdma = em.dma("sync", scr_kn[st * 128:(st + 1) * 128, :], buf[:, :])
                    kn_dma_toks.append(tdma)
                else:
                    em.wait("vector", t_mm)
                    em.wait("vector", t_vinit)
                    base = st * (H * 65)
                    t_vv = em.on("vector", lambda E, base=base, pid=pid: E.tensor_copy(
                        v_aug[:, base:base + H * 65].rearrange("p (h e) -> p h e", e=65)[:, :, 0:64],
                        psum_kv[pid][:, :].rearrange("p (h e) -> p h e", e=64)), inc="cV")
                    psum_free[pid] = t_vv
                    t_vaug_last = t_vv
            if which == "k":
                em.waits("sync", kn_dma_toks)
                t_knT = [em.dmat(knT[:, c * S:(c + 1) * S], scr_kn[:, c * 128:(c + 1) * 128])
                         for c in range(NK)]

    em.barrier()

    hY = 16
    with ExitStack() as ph:
        x_buf = [sb(ph, f"xh{i}", [128, D], F32) for i in range(2)]
        ln_buf = [sb(ph, f"lnbuf{i}", [128, D], BF16) for i in range(2)]
        sqs = sb(ph, "sqs", [128, D], F32)

        ln1_toks = []
        free_x = [None, None]
        free_ln = [None, None]
        for t in range(NT):
            xb = x_buf[t % 2]
            em.wait("sync", free_x[t % 2])
            t_x = em.dma("sync", xb[:, :], x_src[t])
            em.wait("vector", t_x)
            t_redh = em.on("vector", lambda E, xb=xb: E.tensor_reduce(tiny[:, hY:hY + 1], xb[:, :], AX.X, OP.add), inc="cV")
            em.wait("vector", t_redh)
            t_s1 = em.vt(lambda E: E.tensor_scalar(
                tiny[:, hY + 1:hY + 2], tiny[:, hY:hY + 1], 1.0 / D, None, OP.mult))
            em.wait("scalar", t_x)
            t_ssq = em.on("scalar", lambda E, xb=xb: E.activation(
                sqs[:, :], xb[:, :], AF.Square, accum_out=tiny[:, hY + 2:hY + 3]), inc="cS")
            em.waits("vector", [t_ssq, t_s1])
            em.vt(lambda E: E.tensor_scalar(tiny[:, hY + 3:hY + 4], tiny[:, hY + 2:hY + 3], 1.0 / D, None, OP.mult))
            em.vt(lambda E: E.tensor_tensor(tiny[:, hY + 4:hY + 5], tiny[:, hY + 1:hY + 2], tiny[:, hY + 1:hY + 2], OP.mult))
            em.vt(lambda E: E.tensor_tensor(tiny[:, hY + 5:hY + 6], tiny[:, hY + 3:hY + 4], tiny[:, hY + 4:hY + 5], OP.subtract))
            t_var = em.vt(lambda E: E.tensor_scalar(tiny[:, hY + 5:hY + 6], tiny[:, hY + 5:hY + 6], LN_EPS1, None, OP.add))
            em.wait("scalar", t_var)
            em.on("scalar", lambda E: E.activation(tiny[:, hY + 6:hY + 7], tiny[:, hY + 5:hY + 6], AF.Ln))
            t_rstd = em.on("scalar", lambda E, t=t: E.activation(
                al1c[:, t:t + 1], tiny[:, hY + 6:hY + 7], AF.Exp, scale=-0.5), inc="cS")
            em.wait("vector", t_rstd)
            em.vt(lambda E: E.tensor_scalar(tiny[:, hY + 7:hY + 8], tiny[:, hY + 1:hY + 2], -1.0, None, OP.mult))
            t_beta = em.vt(lambda E, t=t: E.tensor_tensor(
                be1c[:, t:t + 1], tiny[:, hY + 7:hY + 8], al1c[:, t:t + 1], OP.mult))
            em.wait("scalar", t_beta)
            em.wait("scalar", free_ln[t % 2])
            lb = ln_buf[t % 2]
            t_lncore = em.on("scalar", lambda E, xb=xb, lb=lb, t=t: E.activation(
                lb[:, :], xb[:, :], AF.Identity, scale=al1c[:, t:t + 1], bias=be1c[:, t:t + 1]), inc="cS")
            free_x[t % 2] = t_lncore
            em.wait("sync", t_lncore)
            t_lnw = em.dma("sync", scr_ln1[t * 128:(t + 1) * 128, :], lb[:, :])
            free_ln[t % 2] = t_lnw
            ln1_toks.append(t_lnw)

    if debug:
        dal = nc.dram_tensor("dbg_al", [128, NT], F32, kind="ExternalOutput").ap()
        dbe = nc.dram_tensor("dbg_be", [128, NT], F32, kind="ExternalOutput").ap()
        em.waits("sync", ln1_toks)
        em.dma("sync", dal[:, :], al1c[:, :])
        em.dma("sync", dbe[:, :], be1c[:, :])

    em.barrier()

    qY = 24
    with ExitStack() as pq:
        wq_sb = sb(pq, "wq_sb", [128, NK * D], BF16)
        wo_sb = sb(pq, "wo_sb", [128, NK * D], BF16)
        hT_blk = sb(pq, "hT_blk", [128, NK * QB], BF16)
        hT_raw = [sb(pq, f"hTr{i}", [128, QB], BF16) for i in range(2)]
        qnT_blk = sb(pq, "qnT_blk", [128, NK * QB], BF16)
        eq_t = sb(pq, "eq_t", [128, D], F32)
        sq2_t = sb(pq, "sq2_t", [128, D], F32)
        r2_t = sq2_t
        qn_bf = [sb(pq, f"qn_bf{i}", [128, D], BF16) for i in range(2)]
        et_bufs = [sb(pq, f"et{i}", [128, 512], BF16) for i in range(4)]
        rr_bf = [sb(pq, f"rr{i}", [1, 512], BF16) for i in range(2)]
        rrb_sb = sb(pq, "rrb_sb", [64, 512], BF16)
        x_buf2 = [sb(pq, f"xq2_{i}", [128, D], F32) for i in range(2)]
        ep0 = sb(pq, "ep0", [128, D], F32)
        ep1 = eq_t
        ep2 = sb(pq, "ep2", [128, D], F32)
        ep3i = sb(pq, "ep3i", [128, D], I32)
        xq_bf = [sb(pq, f"xqb{i}", [128, D], BF16) for i in range(2)]
        xu_buf = [sb(pq, f"xub{i}", [128, D], F32) for i in range(2)]

        psum_q = ps(pq, "psq", [128, D])
        psum_sc = [ps(pq, f"pssc{i}", [128, 512]) for i in range(2)]
        psum_ctx = [ps(pq, f"psctx{i}", [128, 512]) for i in range(2)]
        psum_bc2 = ps(pq, "psbc2", [64, 512])

        w_rq = wq_d.rearrange("(c p) n -> c p n", p=128)
        w_ro = wo_d.rearrange("(c p) n -> c p n", p=128)
        t_wq = [em.dma("gpsimd", wq_sb[:, ic * D:(ic + 1) * D], w_rq[ic]) for ic in range(NK)]
        t_wo = [em.dma("gpsimd", wo_sb[:, ic * D:(ic + 1) * D], w_ro[ic]) for ic in range(NK)]

        em.waits("tensor", t_wq + t_wo)
        em.waits("scalar", [t_sac, t_cac])

        free_psq = None
        free_psc = [None, None]
        free_pctx = [None, None]
        free_pbc = None
        free_hraw = [None, None]
        free_qnbf = [None, None]
        free_et = [None] * 4
        free_x2 = [None, None]
        free_xqb = [None, None]
        free_xu = [None, None]
        prev_oproj = None
        prev_scores = None
        prev_x2v = None
        xq_dma_toks = []
        xu_dma_toks = []

        for b in range(NBLK):
            rows0 = b * QB
            em.waits("sync", ln1_toks)
            em.wait("scalar", prev_oproj)
            hT_toks = []
            for kc in range(NK):
                em.wait("sync", free_hraw[kc % 2])
                t_tr = em.dmat(hT_raw[kc % 2][:, :], scr_ln1[rows0:rows0 + QB, kc * 128:(kc + 1) * 128])
                em.wait("scalar", t_tr)
                t_aff = em.on("scalar", lambda E, kc=kc: E.activation(
                    hT_blk[:, kc * QB:(kc + 1) * QB], hT_raw[kc % 2][:, :], AF.Identity,
                    scale=ca1_col[:, kc:kc + 1], bias=sa_col[:, kc:kc + 1]), inc="cS")
                free_hraw[kc % 2] = t_aff
                hT_toks.append(t_aff)
            em.waits("tensor", hT_toks)
            for tt_ in range(4):
                em.wait("tensor", free_psq)
                for ic in range(NK):
                    stat = hT_blk[:, ic * QB + tt_ * 128: ic * QB + tt_ * 128 + 128]
                    for oc in range(2):
                        em.on("tensor", lambda E, stat=stat, ic=ic, oc=oc: E.matmul(
                            psum_q[:, oc * 512:(oc + 1) * 512], stat,
                            wq_sb[:, ic * D + oc * 512: ic * D + (oc + 1) * 512],
                            start=(ic == 0), stop=False))
                for oc in range(2):
                    t_qmm = em.on("tensor", lambda E, oc=oc: E.matmul(
                        psum_q[:, oc * 512:(oc + 1) * 512], ones_b[0:1, :],
                        bq_bf[0:1, oc * 512:(oc + 1) * 512], start=False, stop=True), inc="cT")
                em.wait("scalar", t_qmm)
                em.on("scalar", lambda E: E.activation(r2_t[:, :], psum_q[:, :], AF.Relu, scale=-1.0))
                t_exp2 = em.on("scalar", lambda E: E.activation(r2_t[:, :], r2_t[:, :], AF.Exp, scale=-1.0), inc="cS")
                em.wait("vector", t_exp2)
                t_eq = em.on("vector", lambda E: E.scalar_tensor_tensor(
                    eq_t[:, :], r2_t[:, :], -1.0, psum_q[:, :], OP.add, OP.max), inc="cV")
                free_psq = t_eq
                em.wait("scalar", t_eq)
                t_sq2 = em.on("scalar", lambda E: E.activation(sq2_t[:, :], eq_t[:, :], AF.Square), inc="cS")
                em.wait("vector", t_sq2)
                t_redq = em.on("vector", lambda E: E.tensor_reduce(
                    tiny[:, qY:qY + H], sq2_t[:, :].rearrange("p (h e) -> p h e", h=H), AX.X, OP.add), inc="cV")
                em.wait("vector", t_redq)
                t_nq = em.on("vector", lambda E: E.tensor_copy(tiny2[:, 16:16 + H], tiny[:, qY:qY + H]), inc="cV")
                em.wait("scalar", t_nq)
                em.on("scalar", lambda E: E.activation(tiny2[:, 16:16 + H], tiny2[:, 16:16 + H], AF.Ln))
                t_rq = em.on("scalar", lambda E: E.activation(tiny2[:, 16:16 + H], tiny2[:, 16:16 + H], AF.Exp, scale=-0.5), inc="cS")
                em.wait("vector", t_rq)
                em.wait("vector", free_qnbf[tt_ % 2])
                qbuf = qn_bf[tt_ % 2]
                t_qn = em.on("vector", lambda E, qbuf=qbuf: E.tensor_tensor(
                    qbuf[:, :].rearrange("p (h e) -> p h e", h=H),
                    eq_t[:, :].rearrange("p (h e) -> p h e", h=H),
                    tiny2[:, 16:16 + H, None].broadcast_to([128, H, 64]), OP.mult), inc="cV")
                em.wait("sync", t_qn)
                rw = rows0 + tt_ * 128
                t_qnw = em.dma("sync", scr_qn[rw:rw + 128, :], qbuf[:, :])
                free_qnbf[tt_ % 2] = t_qnw
            em.wait("sync", t_qnw)
            em.wait("sync", prev_scores)
            t_qnT = [em.dmat(qnT_blk[:, kc * QB:(kc + 1) * QB], scr_qn[rows0:rows0 + QB, kc * 128:(kc + 1) * 128])
                     for kc in range(NK)]
            em.waits("tensor", t_qnT)

            last_cn = None
            for hp in range(NK):
                em.waits("tensor", [free_pctx[0], free_pctx[1]])
                for st in range(NT):
                    eb = (st % 2) * 2
                    for hh in range(2):
                        em.wait("tensor", free_psc[hh])
                        stat = knT[hh * 64:(hh + 1) * 64, hp * S + st * 128: hp * S + st * 128 + 128]
                        mov = qnT_blk[hh * 64:(hh + 1) * 64, hp * QB:(hp + 1) * QB]
                        t_sc = em.on("tensor", lambda E, stat=stat, mov=mov, hh=hh: E.matmul(
                            psum_sc[hh][:, :], stat, mov, start=True, stop=True,
                            tile_position=(hh * 64, 0)), inc="cT")
                    em.wait("scalar", t_sc)
                    for hh in range(2):
                        em.wait("scalar", free_et[eb + hh])
                        t_e = em.on("scalar", lambda E, eb=eb, hh=hh: E.activation(
                            et_bufs[eb + hh][:, :], psum_sc[hh][:, :], AF.Exp, scale=0.125), inc="cS")
                        free_psc[hh] = t_e
                    em.wait("tensor", t_e)
                    for hh in range(2):
                        stat = v_aug[:, st * (H * 65) + (2 * hp + hh) * 65: st * (H * 65) + (2 * hp + hh) * 65 + 65]
                        t_cx = em.on("tensor", lambda E, stat=stat, eb=eb, hh=hh, st=st: E.matmul(
                            psum_ctx[hh][0:65, :], stat, et_bufs[eb + hh][:, :],
                            start=(st == 0), stop=(st == NT - 1)), inc="cT")
                        free_et[eb + hh] = t_cx
                em.wait("vector", t_cx)
                em.wait("vector", prev_oproj if (b > 0 and hp == 0) else None)
                for hh in range(2):
                    def _rr(E, hh=hh):
                        with E.bass.allow_low_precision("softmax normalizer bf16"):
                            return E.reciprocal(rr_bf[hh][0:1, :], psum_ctx[hh][64:65, :])
                    t_rr = em.on("vector", _rr, inc="cV")
                    em.wait("tensor", t_rr)
                    em.wait("tensor", free_pbc)
                    t_bcm = em.on("tensor", lambda E, hh=hh: E.matmul(
                        psum_bc2[:, :], ones_b[0:1, 0:64], rr_bf[hh][0:1, :],
                        start=True, stop=True), inc="cT")
                    em.wait("vector", t_bcm)
                    em.on("vector", lambda E: E.tensor_copy(rrb_sb[:, :], psum_bc2[0:64, :]))
                    t_cn = em.on("vector", lambda E, hh=hh, hp=hp: E.tensor_tensor(
                        hT_blk[hh * 64:(hh + 1) * 64, hp * QB:(hp + 1) * QB],
                        psum_ctx[hh][0:64, :], rrb_sb[:, :], OP.mult), inc="cV")
                    free_pbc = t_cn
                    free_pctx[hh] = t_cn
                    last_cn = t_cn
            em.wait("tensor", last_cn)
            for tt_ in range(4):
                em.wait("tensor", free_psq)
                for cc in range(NK):
                    stat = hT_blk[:, cc * QB + tt_ * 128: cc * QB + tt_ * 128 + 128]
                    for oc in range(2):
                        em.on("tensor", lambda E, stat=stat, cc=cc, oc=oc: E.matmul(
                            psum_q[:, oc * 512:(oc + 1) * 512], stat,
                            wo_sb[:, cc * D + oc * 512: cc * D + (oc + 1) * 512],
                            start=(cc == 0), stop=False))
                for oc in range(2):
                    t_omm = em.on("tensor", lambda E, oc=oc: E.matmul(
                        psum_q[:, oc * 512:(oc + 1) * 512], ones_b[0:1, :],
                        bo_bf[0:1, oc * 512:(oc + 1) * 512], start=False, stop=True), inc="cT")
                prev_oproj = t_omm
                prev_scores = t_omm

                ti_g = b * 4 + tt_
                xb2 = x_buf2[tt_ % 2]
                em.wait("sync", free_x2[tt_ % 2])
                t_x2 = em.dma("sync", xb2[:, :], x_src[ti_g])
                em.wait("scalar", t_x2)
                em.wait("scalar", prev_x2v)
                t_hc = em.on("scalar", lambda E, xb2=xb2, ti_g=ti_g: E.activation(
                    ep0[:, :], xb2[:, :], AF.Identity, scale=al1c[:, ti_g:ti_g + 1], bias=be1c[:, ti_g:ti_g + 1]), inc="cS")
                em.wait("vector", t_hc)
                em.on("vector", lambda E: E.tensor_tensor(ep1[:, :], ep0[:, :], ca1_b[:, :], OP.mult))
                em.on("vector", lambda E: E.tensor_tensor(ep1[:, :], ep1[:, :], sa_b[:, :], OP.add))
                em.wait("vector", t_omm)
                t_x2v = em.on("vector", lambda E: E.scalar_tensor_tensor(
                    ep0[:, :], psum_q[:, :], 1.0, ep1[:, :], OP.mult, OP.add,
                    accum_out=tiny[:, qY:qY + 1]), inc="cV")
                free_psq = t_x2v
                prev_x2v = t_x2v
                em.wait("scalar", t_x2v)
                t_sq3 = em.on("scalar", lambda E: E.activation(
                    sq2_t[:, :], ep0[:, :], AF.Square, accum_out=tiny[:, qY + 2:qY + 3]), inc="cS")
                em.wait("vector", t_sq3)
                em.vt(lambda E: E.tensor_scalar(tiny[:, qY + 1:qY + 2], tiny[:, qY:qY + 1], 1.0 / D, None, OP.mult))
                em.vt(lambda E: E.tensor_scalar(tiny[:, qY + 3:qY + 4], tiny[:, qY + 2:qY + 3], 1.0 / D, None, OP.mult))
                em.vt(lambda E: E.tensor_tensor(tiny[:, qY + 4:qY + 5], tiny[:, qY + 1:qY + 2], tiny[:, qY + 1:qY + 2], OP.mult))
                em.vt(lambda E: E.tensor_tensor(tiny[:, qY + 5:qY + 6], tiny[:, qY + 3:qY + 4], tiny[:, qY + 4:qY + 5], OP.subtract))
                t_v2 = em.vt(lambda E: E.tensor_scalar(tiny[:, qY + 5:qY + 6], tiny[:, qY + 5:qY + 6], LN_EPS2, None, OP.add))
                em.wait("scalar", t_v2)
                em.on("scalar", lambda E: E.activation(tiny[:, qY + 6:qY + 7], tiny[:, qY + 5:qY + 6], AF.Ln))
                t_a2 = em.on("scalar", lambda E: E.activation(tiny[:, qY + 8:qY + 9], tiny[:, qY + 6:qY + 7], AF.Exp, scale=-0.5), inc="cS")
                em.wait("vector", t_a2)
                em.vt(lambda E: E.tensor_scalar(tiny[:, qY + 7:qY + 8], tiny[:, qY + 1:qY + 2], -1.0, None, OP.mult))
                t_b2 = em.vt(lambda E: E.tensor_tensor(
                    tiny[:, qY + 9:qY + 10], tiny[:, qY + 7:qY + 8], tiny[:, qY + 8:qY + 9], OP.mult))
                em.wait("scalar", t_b2)
                t_lc2 = em.on("scalar", lambda E: E.activation(
                    ep2[:, :], ep0[:, :], AF.Identity, scale=tiny[:, qY + 8:qY + 9], bias=tiny[:, qY + 9:qY + 10]), inc="cS")
                em.wait("vector", t_lc2)
                em.on("vector", lambda E: E.tensor_tensor(ep2[:, :], ep2[:, :], g2_b[:, :], OP.mult))
                em.on("vector", lambda E: E.tensor_tensor(ep2[:, :], ep2[:, :], b2_b[:, :], OP.add))
                em.wait("vector", free_xu[tt_ % 2])
                xu_b = xu_buf[tt_ % 2]
                t_xu = em.on("vector", lambda E, xb2=xb2, xu_b=xu_b: E.scalar_tensor_tensor(
                    xu_b[:, :], ep2[:, :], 1.0, xb2[:, :], OP.mult, OP.add,
                    accum_out=tiny[:, qY:qY + 1]), inc="cV")
                free_x2[tt_ % 2] = t_xu
                em.wait("sync", t_xu)
                t_xuw = em.dma("sync", scr_xu[ti_g * 128:(ti_g + 1) * 128, :], xu_b[:, :])
                xu_dma_toks.append(t_xuw)
                em.wait("scalar", t_xu)
                t_sq4 = em.on("scalar", lambda E, xu_b=xu_b: E.activation(
                    sq2_t[:, :], xu_b[:, :], AF.Square, accum_out=tiny[:, qY + 2:qY + 3]), inc="cS")
                em.wait("vector", t_sq4)
                em.vt(lambda E: E.tensor_scalar(tiny[:, qY + 1:qY + 2], tiny[:, qY:qY + 1], 1.0 / D, None, OP.mult))
                em.vt(lambda E: E.tensor_scalar(tiny[:, qY + 3:qY + 4], tiny[:, qY + 2:qY + 3], 1.0 / D, None, OP.mult))
                em.vt(lambda E: E.tensor_tensor(tiny[:, qY + 4:qY + 5], tiny[:, qY + 1:qY + 2], tiny[:, qY + 1:qY + 2], OP.mult))
                em.vt(lambda E: E.tensor_tensor(tiny[:, qY + 5:qY + 6], tiny[:, qY + 3:qY + 4], tiny[:, qY + 4:qY + 5], OP.subtract))
                t_v3 = em.vt(lambda E: E.tensor_scalar(tiny[:, qY + 5:qY + 6], tiny[:, qY + 5:qY + 6], LN_EPS1, None, OP.add))
                em.wait("scalar", t_v3)
                em.on("scalar", lambda E: E.activation(tiny[:, qY + 6:qY + 7], tiny[:, qY + 5:qY + 6], AF.Ln))
                t_a3 = em.on("scalar", lambda E: E.activation(tiny[:, qY + 8:qY + 9], tiny[:, qY + 6:qY + 7], AF.Exp, scale=-0.5), inc="cS")
                em.wait("vector", t_a3)
                em.vt(lambda E: E.tensor_scalar(tiny[:, qY + 7:qY + 8], tiny[:, qY + 1:qY + 2], -1.0, None, OP.mult))
                t_b3 = em.vt(lambda E: E.tensor_tensor(
                    tiny[:, qY + 9:qY + 10], tiny[:, qY + 7:qY + 8], tiny[:, qY + 8:qY + 9], OP.mult))
                em.wait("scalar", t_b3)
                t_lc3 = em.on("scalar", lambda E, xu_b=xu_b: E.activation(
                    ep2[:, :], xu_b[:, :], AF.Identity, scale=tiny[:, qY + 8:qY + 9], bias=tiny[:, qY + 9:qY + 10]), inc="cS")
                em.wait("vector", t_lc3)
                em.on("vector", lambda E: E.tensor_tensor(ep2[:, :], ep2[:, :], cb1_b[:, :], OP.mult))
                em.on("vector", lambda E: E.tensor_tensor(ep2[:, :], ep2[:, :], sb_b[:, :], OP.add))
                t_redm = em.on("vector", lambda E: E.tensor_reduce(tiny[:, qY:qY + 1], ep2[:, :], AX.X, OP.max,
                                                          apply_absolute_value=True), inc="cV")
                em.wait("vector", t_redm)
                em.vt(lambda E: E.tensor_scalar(tiny[:, qY:qY + 1], tiny[:, qY:qY + 1], 1e-5, None, OP.max))
                em.vt(lambda E, ti_g=ti_g: E.tensor_scalar(
                    rs1c[:, ti_g:ti_g + 1], tiny[:, qY:qY + 1], 1.0 / 127.0, None, OP.mult))
                em.vt(lambda E: E.reciprocal(tiny[:, qY + 1:qY + 2], tiny[:, qY:qY + 1]))
                em.vt(lambda E, ti_g=ti_g: E.tensor_scalar(
                    s1nc[:, ti_g:ti_g + 1], tiny[:, qY + 1:qY + 2], -127.0, None, OP.mult))
                t_zp = em.on("vector", lambda E, ti_g=ti_g: E.tensor_scalar(
                    ep2[:, :], ep2[:, :], s1nc[:, ti_g:ti_g + 1], 128.0, OP.mult, OP.add), inc="cV")
                em.wait("scalar", t_zp)
                t_cast = em.on("scalar", lambda E: E.activation(ep3i[:, :], ep2[:, :], AF.Copy), inc="cS")
                em.wait("vector", t_cast)
                em.on("vector", lambda E: E.scalar_tensor_tensor(
                    ep2[:, :], ep2[:, :], -0.5, ep3i[:, :], OP.add, OP.is_ge))
                em.wait("vector", free_xqb[tt_ % 2])
                xqb = xq_bf[tt_ % 2]
                t_xq = em.on("vector", lambda E, xqb=xqb: E.scalar_tensor_tensor(
                    xqb[:, :], ep2[:, :], -128.0, ep3i[:, :], OP.add, OP.add), inc="cV")
                em.wait("sync", t_xq)
                t_xqw = em.dma("sync", scr_xq[ti_g * 128:(ti_g + 1) * 128, :], xqb[:, :])
                free_xqb[tt_ % 2] = t_xqw
                free_xu[tt_ % 2] = t_xqw
                xq_dma_toks.append(t_xqw)

    attn.close()
    mid.close()
    em.barrier()

    wY = 40
    with ExitStack() as pw:
        wbuf = [sb(pw, f"wb{i}", [128, 4 * D], F32) for i in range(2)]
        wabs = sb(pw, "wabs", [128, 4 * D], F32)
        tern = [sb(pw, f"tern{i}", [128, 4 * D], BF16) for i in range(2)]
        asum = sb(pw, "asum", [128, 8], F32)
        psum_m = ps(pw, "psm", [1, 1])
        psum_mb = ps(pw, "psmb", [128, 1])

        tern_w_toks = {"wg": [], "wu": [], "wd": []}
        t_mc = None
        for wname, w_in, w_scr, mcol, nrow in (
                ("wg", wg_d, scr_wg, mwg_c, INNER),
                ("wu", wu_d, scr_wu, mwu_c, INNER),
                ("wd", wd_d, scr_wd, mwd_c, D)):
            ngrp = (nrow * (INNER * D // nrow // D) * D) // (128 * 4 * D)
            ngrp = (nrow * (D if w_in.shape[1] == D else INNER)) // (128 * 4 * D)
            w_flat = w_in.rearrange("a b -> (a b)").rearrange("(g p n) -> g p n", p=128, n=4 * D)
            scr_flat = w_scr.rearrange("a b -> (a b)").rearrange("(g p n) -> g p n", p=128, n=4 * D)
            t_grp = [None, None]
            last_abs = None
            for g in range(ngrp):
                wb = wbuf[g % 2]
                em.wait("sync", t_grp[g % 2])
                t_l = em.dma("sync", wb[:, :], w_flat[g])
                em.wait("scalar", t_l)
                t_abs = em.on("scalar", lambda E, wb=wb, g=g: E.activation(
                    wabs[:, :], wb[:, :], AF.Abs, accum_out=asum[:, g:g + 1]), inc="cS")
                t_grp[g % 2] = t_abs
                last_abs = t_abs
            em.wait("vector", last_abs)
            t_redw = em.on("vector", lambda E, ngrp=ngrp: E.tensor_reduce(tiny[:, wY:wY + 1], asum[:, 0:ngrp], AX.X, OP.add), inc="cV")
            em.wait("vector", t_redw)
            t_cs = em.on("vector", lambda E: E.tensor_copy(tiny2[:, 24:25], tiny[:, wY:wY + 1]), inc="cV")
            em.wait("tensor", t_cs)
            t_ms = em.on("tensor", lambda E: E.matmul(psum_m[0:1, 0:1], ones_c[:, 0:1], tiny2[:, 24:25],
                                                      start=True, stop=True), inc="cT")
            em.wait("vector", t_ms)
            t_mean = em.vt(lambda E, ngrp=ngrp: E.tensor_scalar(
                trow[0:1, 0:1], psum_m[0:1, 0:1], 1.0 / (ngrp * 128 * 4 * D), None, OP.mult))
            em.wait("tensor", t_mean)
            t_mb = em.on("tensor", lambda E: E.matmul(psum_mb[:, 0:1], ones_f[0:1, :], trow[0:1, 0:1],
                                                      start=True, stop=True), inc="cT")
            em.wait("vector", t_mb)
            t_mc = em.vt(lambda E, mcol=mcol: E.tensor_copy(mcol[:, :], psum_mb[:, :]))
            em.vt(lambda E, mcol=mcol: E.tensor_scalar(tiny[:, wY + 1:wY + 2], mcol[:, :], 0.5, None, OP.mult))
            t_thr = em.vt(lambda E, mcol=mcol: E.tensor_scalar(tiny[:, wY + 2:wY + 3], mcol[:, :], -0.5, None, OP.mult))
            t_tw = [None, None]
            for g in range(ngrp):
                wb = wbuf[g % 2]
                em.wait("sync", t_grp[g % 2])
                t_l = em.dma("sync", wb[:, :], w_flat[g])
                em.waits("vector", [t_l, t_thr, t_tw[g % 2]])
                em.on("vector", lambda E, wb=wb: E.tensor_scalar(wabs[:, :], wb[:, :], tiny[:, wY + 1:wY + 2], None, OP.is_ge))
                tb = tern[g % 2]
                t_t = em.on("vector", lambda E, wb=wb, tb=tb: E.scalar_tensor_tensor(
                    tb[:, :], wb[:, :], tiny[:, wY + 2:wY + 3], wabs[:, :], OP.is_le, OP.subtract), inc="cV")
                t_grp[g % 2] = t_t
                em.wait("sync", t_t)
                t_w = em.dma("sync", scr_flat[g], tb[:, :])
                t_tw[g % 2] = t_w
                tern_w_toks[wname].append(t_w)

    em.barrier()

    fY = 48
    with ExitStack() as pf:
        wgT = sb(pf, "wgT", [128, NK * INNER], BF16)
        wuT = sb(pf, "wuT", [128, NK * INNER], BF16)
        xqT_h = sb(pf, "xqT_h", [128, NK * 1024], BF16)
        gu = sb(pf, "gu", [128, INNER], F32)
        gti = sb(pf, "gti", [128, INNER], I32)
        guq = sb(pf, "guq", [128, INNER], BF16)
        sil = sb(pf, "sil", [128, 512], F32)
        psum_g = [ps(pf, f"psg{i}", [128, 512]) for i in range(2)]
        psum_u = [ps(pf, f"psu{i}", [128, 512]) for i in range(2)]

        em.waits("sync", tern_w_toks["wg"] + tern_w_toks["wu"])
        t_wgT = [em.dmat(wgT[:, ic * INNER:(ic + 1) * INNER], scr_wg[:, ic * 128:(ic + 1) * 128])
                 for ic in range(NK)]
        t_wuT = [em.dmat(wuT[:, ic * INNER:(ic + 1) * INNER], scr_wu[:, ic * 128:(ic + 1) * 128])
                 for ic in range(NK)]
        em.waits("tensor", t_wgT + t_wuT)
        em.waits("vector", [t_mc])

        guq_w_toks = []
        free_xqT = None
        free_guq = None
        free_psg = [None, None]
        free_psu = [None, None]
        prev_sil = None
        t_fmm = None
        for half in range(2):
            em.wait("sync", free_xqT)
            em.waits("sync", xq_dma_toks)
            rows0 = half * 1024
            t_xqT = [em.dmat(xqT_h[:, ic * 1024:(ic + 1) * 1024],
                             scr_xq[rows0:rows0 + 1024, ic * 128:(ic + 1) * 128])
                     for ic in range(NK)]
            em.waits("tensor", t_xqT)
            for t8 in range(8):
                ti_g = half * 8 + t8
                em.wait("vector", prev_sil)
                em.vt(lambda E, ti_g=ti_g: E.tensor_tensor(
                    tiny[:, fY:fY + 1], rs1c[:, ti_g:ti_g + 1], mwg_c[:, :], OP.mult))
                t_al = em.vt(lambda E, ti_g=ti_g: E.tensor_tensor(
                    tiny[:, fY + 1:fY + 2], rs1c[:, ti_g:ti_g + 1], mwu_c[:, :], OP.mult))
                em.wait("scalar", t_al)
                for oc in range(NK):
                    pg = psum_g[oc % 2]
                    pu = psum_u[oc % 2]
                    em.waits("tensor", [free_psg[oc % 2], free_psu[oc % 2]])
                    for ic in range(NK):
                        stat = xqT_h[:, ic * 1024 + t8 * 128: ic * 1024 + t8 * 128 + 128]
                        em.on("tensor", lambda E, stat=stat, ic=ic, oc=oc, pg=pg: E.matmul(
                            pg[:, :], stat, wgT[:, ic * INNER + oc * 512: ic * INNER + (oc + 1) * 512],
                            start=(ic == 0), stop=(ic == NK - 1)))
                        t_fmm = em.on("tensor", lambda E, stat=stat, ic=ic, oc=oc, pu=pu: E.matmul(
                            pu[:, :], stat, wuT[:, ic * INNER + oc * 512: ic * INNER + (oc + 1) * 512],
                            start=(ic == 0), stop=(ic == NK - 1)), inc="cT")
                    em.wait("scalar", t_fmm)
                    t_sil = em.on("scalar", lambda E, pg=pg: E.activation(
                        sil[:, :], pg[:, :], AF.Silu, scale=tiny[:, fY:fY + 1]), inc="cS")
                    free_psg[oc % 2] = t_sil
                    prev_sil = t_sil
                    em.wait("vector", t_sil)
                    t_gu = em.on("vector", lambda E, pu=pu, oc=oc: E.scalar_tensor_tensor(
                        gu[:, oc * 512:(oc + 1) * 512], pu[:, :], tiny[:, fY + 1:fY + 2], sil[:, :],
                        OP.mult, OP.mult), inc="cV")
                    free_psu[oc % 2] = t_gu
                t_redf = em.on("vector", lambda E: E.tensor_reduce(tiny[:, fY + 2:fY + 3], gu[:, :], AX.X, OP.max,
                                                          apply_absolute_value=True), inc="cV")
                em.wait("vector", t_redf)
                em.on("vector", lambda E: E.tensor_scalar(tiny[:, fY + 2:fY + 3], tiny[:, fY + 2:fY + 3], 1e-5, None, OP.max))
                em.on("vector", lambda E, ti_g=ti_g: E.tensor_scalar(
                    rs2c[:, ti_g:ti_g + 1], tiny[:, fY + 2:fY + 3], 1.0 / 127.0, None, OP.mult))
                em.on("vector", lambda E: E.reciprocal(tiny[:, fY + 3:fY + 4], tiny[:, fY + 2:fY + 3]))
                em.on("vector", lambda E: E.tensor_scalar(tiny[:, fY + 3:fY + 4], tiny[:, fY + 3:fY + 4], -127.0, None, OP.mult))
                t_zp2 = em.on("vector", lambda E: E.tensor_scalar(
                    gu[:, :], gu[:, :], tiny[:, fY + 3:fY + 4], 128.0, OP.mult, OP.add), inc="cV")
                em.wait("scalar", t_zp2)
                t_cast2 = em.on("scalar", lambda E: E.activation(gti[:, :], gu[:, :], AF.Copy), inc="cS")
                em.wait("vector", t_cast2)
                em.on("vector", lambda E: E.scalar_tensor_tensor(
                    gu[:, :], gu[:, :], -0.5, gti[:, :], OP.add, OP.is_ge))
                em.wait("vector", free_guq)
                t_gq = em.on("vector", lambda E: E.scalar_tensor_tensor(
                    guq[:, :], gu[:, :], -128.0, gti[:, :], OP.add, OP.add), inc="cV")
                em.wait("sync", t_gq)
                t_gw = em.dma("sync", scr_guq[ti_g * 128:(ti_g + 1) * 128, :], guq[:, :])
                free_guq = t_gw
                guq_w_toks.append(t_gw)
            free_xqT = t_fmm

    em.barrier()

    dY = 56
    with ExitStack() as pd:
        wdT = sb(pd, "wdT", [128, 32 * D], BF16)
        gqT = [sb(pd, f"gqT{i}", [128, 32 * 512], BF16) for i in range(2)]
        od0 = sb(pd, "od0", [128, D], F32)
        od1 = sb(pd, "od1", [128, D], F32)
        xu_r = [sb(pd, f"xur{i}", [128, D], F32) for i in range(2)]
        psum_d = [ps(pd, f"psd{i}", [128, D]) for i in range(2)]

        em.waits("sync", tern_w_toks["wd"])
        t_wdT = [em.dmat(wdT[:, jc * D:(jc + 1) * D], scr_wd[:, jc * 128:(jc + 1) * 128])
                 for jc in range(32)]
        em.waits("tensor", t_wdT)

        out_toks = []
        free_gqT = [None, None]
        free_psd = [None, None]
        free_xur = [None, None]
        prev_m1 = None
        prev_ty = None
        for blk in range(NBLK):
            rows0 = blk * QB
            em.wait("sync", free_gqT[blk % 2])
            em.waits("sync", guq_w_toks)
            gq_b = gqT[blk % 2]
            t_gqT = [em.dmat(gq_b[:, jc * 512:(jc + 1) * 512],
                             scr_guq[rows0:rows0 + QB, jc * 128:(jc + 1) * 128])
                     for jc in range(32)]
            em.waits("tensor", t_gqT)
            t_dmm = None
            for tt_ in range(4):
                ti_g = blk * 4 + tt_
                pd_id = tt_ % 2
                em.wait("tensor", free_psd[pd_id])
                for jc in range(32):
                    stat = gq_b[:, jc * 512 + tt_ * 128: jc * 512 + tt_ * 128 + 128]
                    for oc in range(2):
                        t_dmm = em.on("tensor", lambda E, stat=stat, jc=jc, oc=oc, pd_id=pd_id: E.matmul(
                            psum_d[pd_id][:, oc * 512:(oc + 1) * 512], stat,
                            wdT[:, jc * D + oc * 512: jc * D + (oc + 1) * 512],
                            start=(jc == 0), stop=(jc == 31)),
                            inc=("cT" if (jc == 31 and oc == 1) else None))
                em.wait("vector", prev_m1)
                t_ad = em.vt(lambda E, ti_g=ti_g: E.tensor_tensor(
                    tiny[:, dY:dY + 1], rs2c[:, ti_g:ti_g + 1], mwd_c[:, :], OP.mult))
                em.waits("scalar", [t_dmm, t_ad, prev_m1])
                t_od = em.on("scalar", lambda E, pd_id=pd_id: E.activation(
                    od0[:, :], psum_d[pd_id][:, :], AF.Identity, scale=tiny[:, dY:dY + 1]), inc="cS")
                free_psd[pd_id] = t_od
                em.wait("sync", free_xur[tt_ % 2])
                em.wait("sync", xu_dma_toks[ti_g])
                xur = xu_r[tt_ % 2]
                t_xur = em.dma("sync", xur[:, :], scr_xu[ti_g * 128:(ti_g + 1) * 128, :])
                em.waits("vector", [t_od, t_xur, prev_ty])
                t_m1 = em.on("vector", lambda E: E.tensor_tensor(od1[:, :], od0[:, :], gb_b[:, :], OP.mult), inc="cV")
                prev_m1 = t_m1
                t_out = em.on("vector", lambda E, xur=xur: E.tensor_tensor(od1[:, :], od1[:, :], xur[:, :], OP.add), inc="cV")
                free_xur[tt_ % 2] = t_out
                em.wait("sync", t_out)
                t_y = em.dma("sync", y_d[ti_g * 128:(ti_g + 1) * 128, :], od1[:, :])
                prev_ty = t_y
                out_toks.append(t_y)
            free_gqT[blk % 2] = t_dmm

        em.waits("sync", out_toks)

    with nc.Block() as block:
        @block.sync
        def _(E):
            for f in em.ops["sync"]:
                f(E)

        @block.gpsimd
        def _(E):
            for f in em.ops["gpsimd"]:
                f(E)

        @block.tensor
        def _(E):
            for f in em.ops["tensor"]:
                f(E)

        @block.vector
        def _(E):
            for f in em.ops["vector"]:
                f(E)

        @block.scalar
        def _(E):
            for f in em.ops["scalar"]:
                f(E)

    es.close()
    return nc


_CACHE = {}


def kernel(**inputs):
    def g(name):
        return np.ascontiguousarray(np.asarray(inputs[name], dtype=np.float32))

    x = g("L0_attn_target_embedding").reshape(16384, D)
    c = g("L0_c")
    shared = {
        "src": g("L0_attn_source_embedding"),
        "val": g("L0_attn_value_embedding"),
        "wq": g("L0_attn_Wq"), "wk": g("L0_attn_Wk"), "wv": g("L0_attn_Wv"), "wo": g("L0_attn_Wo"),
        "bq": g("L0_attn_bq"), "bk": g("L0_attn_bk"), "bv": g("L0_attn_bv"), "bo": g("L0_attn_bo"),
        "lng": g("L0_attn_ln_g"), "lnb": g("L0_attn_ln_b"),
        "wg": g("L0_ffn_w_gate"), "wu": g("L0_ffn_w_up"), "wd": g("L0_ffn_w_down"),
        "wmod": g("L0_wmod"),
    }
    if "nc" not in _CACHE:
        _CACHE["nc"] = build()
    nc = _CACHE["nc"]
    in_maps = []
    for core in range(8):
        m = dict(shared)
        m["x"] = np.ascontiguousarray(x[core * T:(core + 1) * T])
        m["cb"] = np.ascontiguousarray(c[core // 4])
        in_maps.append(m)
    res = run_bass_kernel_spmd(nc, in_maps, core_ids=list(range(8)))
    y = np.concatenate([np.asarray(res.results[i]["y"]) for i in range(8)], axis=0)
    return y.reshape(2, 8192, D).astype(np.float32)


# version 05
# speedup vs baseline: 1.0101x; 1.0101x over previous
import numpy as np
from contextlib import ExitStack

import concourse.bass as bass
import concourse.mybir as mybir
from concourse.bass_utils import run_bass_kernel_spmd

F32 = mybir.dt.float32
BF16 = mybir.dt.bfloat16
I32 = mybir.dt.int32
AF = mybir.ActivationFunctionType
OP = mybir.AluOpType
AX = mybir.AxisListType

D = 1024
T = 2048
S = 2048
H = 16
E = 64
NT = 16
QB = 512
NBLK = 4
INNER = 4096
NK = 8
LN_EPS1 = 1e-5
LN_EPS2 = 1e-12

ENGS = ("sync", "gpsimd", "tensor", "vector", "scalar")


class Em:
    def __init__(self):
        self.ops = {e: [] for e in ENGS}
        self.cnt = {s: 0 for s in ("dS", "dG", "cT", "cV", "cS")}
        self.h = {}

    def on(self, eng, fn, inc=None, inc_by=1):
        tok = None
        if inc is not None:
            self.cnt[inc] += inc_by
            tok = (inc, self.cnt[inc])

            def wrapped(E, fn=fn, semn=inc, inc_by=inc_by):
                fn(E).then_inc(self.h[semn], inc_by)
            self.ops[eng].append(wrapped)
        else:
            self.ops[eng].append(fn)
        return tok

    def dma(self, eng, out, in_, slow=False):
        semn = "dG" if eng == "gpsimd" else "dS"

        def fn(E, out=out, in_=in_, slow=slow):
            if slow:
                with E.bass.allow_non_contiguous_dma("small const load"):
                    return E.dma_start(out, in_)
            return E.dma_start(out, in_)
        return self.on(eng, fn, inc=semn, inc_by=16)

    def dmat(self, out, in_):
        return self.on("sync", lambda E, out=out, in_=in_: E.dma_start_transpose(out, in_),
                       inc="dS", inc_by=16)

    def wait(self, eng, tok):
        if tok is None:
            return
        semn, v = tok
        self.ops[eng].append(lambda E, semn=semn, v=v: E.wait_ge(self.h[semn], v))

    def waits(self, eng, toks):
        best = {}
        for t in toks:
            if t is None:
                continue
            s, v = t
            best[s] = max(best.get(s, 0), v)
        for s, v in best.items():
            self.wait(eng, (s, v))

    def vt(self, fn):
        tok = self.on("vector", fn, inc="cV")
        self.wait("vector", tok)
        return tok

    def barrier(self):
        toks = [(s, c) for s, c in self.cnt.items() if c > 0]
        for e in ENGS:
            self.waits(e, toks)


def build(debug=False):
    nc = bass.Bass("TRN2", target_bir_lowering=False, detect_race_conditions=False)

    x_d = nc.dram_tensor("x", [T, D], F32, kind="ExternalInput").ap()
    src_d = nc.dram_tensor("src", [S, D], F32, kind="ExternalInput").ap()
    val_d = nc.dram_tensor("val", [S, D], F32, kind="ExternalInput").ap()
    wq_d = nc.dram_tensor("wq", [D, D], F32, kind="ExternalInput").ap()
    wk_d = nc.dram_tensor("wk", [D, D], F32, kind="ExternalInput").ap()
    wv_d = nc.dram_tensor("wv", [D, D], F32, kind="ExternalInput").ap()
    wo_d = nc.dram_tensor("wo", [D, D], F32, kind="ExternalInput").ap()
    bq_d = nc.dram_tensor("bq", [D], F32, kind="ExternalInput").ap()
    bk_d = nc.dram_tensor("bk", [D], F32, kind="ExternalInput").ap()
    bv_d = nc.dram_tensor("bv", [D], F32, kind="ExternalInput").ap()
    bo_d = nc.dram_tensor("bo", [D], F32, kind="ExternalInput").ap()
    lng_d = nc.dram_tensor("lng", [D], F32, kind="ExternalInput").ap()
    lnb_d = nc.dram_tensor("lnb", [D], F32, kind="ExternalInput").ap()
    wg_d = nc.dram_tensor("wg", [INNER, D], F32, kind="ExternalInput").ap()
    wu_d = nc.dram_tensor("wu", [INNER, D], F32, kind="ExternalInput").ap()
    wd_d = nc.dram_tensor("wd", [D, INNER], F32, kind="ExternalInput").ap()
    cb_d = nc.dram_tensor("cb", [D], F32, kind="ExternalInput").ap()
    wmod_d = nc.dram_tensor("wmod", [D, 6 * D], F32, kind="ExternalInput").ap()
    y_d = nc.dram_tensor("y", [T, D], F32, kind="ExternalOutput").ap()

    KIND = "ExternalOutput" if debug else "Internal"
    scr_src = nc.dram_tensor("scr_src", [S, D], BF16, kind=KIND).ap()
    scr_val = nc.dram_tensor("scr_val", [S, D], BF16, kind=KIND).ap()
    scr_ln1 = nc.dram_tensor("scr_ln1", [T, D], BF16, kind=KIND).ap()
    scr_qn = nc.dram_tensor("scr_qn", [T, D], BF16, kind=KIND).ap()
    scr_kn = nc.dram_tensor("scr_kn", [S, D], BF16, kind=KIND).ap()
    scr_xq = nc.dram_tensor("scr_xq", [T, D], BF16, kind=KIND).ap()
    scr_guq = nc.dram_tensor("scr_guq", [T, INNER], BF16, kind=KIND).ap()
    scr_xu = nc.dram_tensor("scr_xu", [T, D], F32, kind=KIND).ap()
    scr_wg = nc.dram_tensor("scr_wg", [INNER, D], BF16, kind=KIND).ap()
    scr_wu = nc.dram_tensor("scr_wu", [INNER, D], BF16, kind=KIND).ap()
    scr_wd = nc.dram_tensor("scr_wd", [D, INNER], BF16, kind=KIND).ap()
    scr_rows = nc.dram_tensor("scr_rows", [2, D], F32).ap()

    em = Em()
    es = ExitStack()
    for s in ("dS", "dG", "cT", "cV", "cS"):
        em.h[s] = es.enter_context(nc.semaphore(s))

    def sb(stk, name, shape, dt):
        return stk.enter_context(nc.sbuf_tensor(name, shape, dt))

    def ps(stk, name, shape):
        return stk.enter_context(nc.psum_tensor(name, shape, F32))

    ones_f = sb(es, "ones_f", [1, 128], F32)
    ones_c = sb(es, "ones_c", [128, 1], F32)
    ones_b = sb(es, "ones_b", [1, 128], BF16)
    c_cols = sb(es, "c_cols", [128, 8], F32)
    c_bf = sb(es, "c_bf", [128, 8], BF16)
    bq_bf = sb(es, "bq_bf", [1, D], BF16)
    bk_bf = sb(es, "bk_bf", [1, D], BF16)
    bv_bf = sb(es, "bv_bf", [1, D], BF16)
    bo_bf = sb(es, "bo_bf", [1, D], BF16)
    gb_b = sb(es, "gb_b", [128, D], F32)
    al1c = sb(es, "al1c", [128, NT], F32)
    be1c = sb(es, "be1c", [128, NT], F32)
    rs1c = sb(es, "rs1c", [128, NT], F32)
    s1nc = sb(es, "s1nc", [128, NT], F32)
    rs2c = sb(es, "rs2c", [128, NT], F32)
    mwg_c = sb(es, "mwg_c", [128, 1], F32)
    mwu_c = sb(es, "mwu_c", [128, 1], F32)
    mwd_c = sb(es, "mwd_c", [128, 1], F32)
    tiny = sb(es, "tiny", [128, 64], F32)
    tiny2 = sb(es, "tiny2", [128, 32], F32)
    trow = sb(es, "trow", [1, 16], F32)

    mid = ExitStack()
    g2_b = sb(mid, "g2_b", [128, D], F32)
    b2_b = sb(mid, "b2_b", [128, D], F32)
    cb1_b = sb(mid, "cb1_b", [128, D], F32)
    sb_b = sb(mid, "sb_b", [128, D], F32)
    ca1_b = sb(mid, "ca1_b", [128, D], F32)
    sa_b = sb(mid, "sa_b", [128, D], F32)
    ca1_col = sb(mid, "ca1_col", [128, 8], F32)
    sa_col = sb(mid, "sa_col", [128, 8], F32)

    attn = ExitStack()
    knT = sb(attn, "knT", [128, NK * S], BF16)
    v_aug = sb(attn, "v_aug", [128, NT * H * 65], BF16)

    x_src = x_d.rearrange("(t p) d -> t p d", p=128)

    t_c = em.dma("sync", c_cols[:, :], cb_d.rearrange("(k p) -> p k", p=128), slow=True)
    em.on("vector", lambda E: E.memset(ones_f[:, :], 1.0))
    em.on("vector", lambda E: E.memset(ones_c[:, :], 1.0))
    t_ones = em.on("vector", lambda E: E.memset(ones_b[:, :], 1.0), inc="cV")
    t_bias = []
    for buf, bsrc in ((bq_bf, bq_d), (bk_bf, bk_d), (bv_bf, bv_d), (bo_bf, bo_d)):
        t_bias.append(em.dma("gpsimd", buf[:, :], bsrc[None, :]))
    em.wait("scalar", t_c)
    for _f in (AF.Silu, AF.Relu, AF.Exp, AF.Square, AF.Ln, AF.Abs, AF.Identity):
        em.on("scalar", lambda E, _f=_f: E.activation(tiny2[:, 28:29], c_cols[:, 0:1], _f,
                                                      **({"scale": 0.0, "bias": 1.0} if _f == AF.Ln else {})))
    t_cbf = em.on("scalar", lambda E: E.activation(c_bf[:, :], c_cols[:, :], AF.Silu), inc="cS")

    with ExitStack() as p0:
        lng_row = sb(p0, "lng_row", [1, D], F32)
        lnb_row = sb(p0, "lnb_row", [1, D], F32)
        mod_row = sb(p0, "mod_row", [1, 6 * D], F32)
        drv = sb(p0, "drv", [1, 2 * D], F32)
        tmp_row = sb(p0, "tmp_row", [1, 2 * D], F32)
        wmod_buf = [sb(p0, f"wmod{i}", [128, 6 * D], BF16) for i in range(2)]
        psum_mod = [ps(p0, f"psmod{i}", [1, 512]) for i in range(6)]
        psum_bc = ps(p0, "psbc", [128, 512])

        t_lng = em.dma("sync", lng_row[:, :], lng_d[None, :])
        t_lnb = em.dma("sync", lnb_row[:, :], lnb_d[None, :])

        wmod_src = wmod_d.rearrange("(c p) n -> c p n", p=128)
        em.wait("tensor", t_cbf)
        em.wait("tensor", t_ones)
        ev_toks = []
        wmod_war = [None, None]
        for p2 in range(2):
            for kc in range(NK):
                buf = wmod_buf[kc % 2]
                em.wait("gpsimd", wmod_war[kc % 2])
                tl = em.dma("gpsimd", buf[:, :], wmod_src[kc])
                em.wait("tensor", tl)
                for ncn in range(6):
                    col0 = (p2 * 6 + ncn) * 512
                    tok = em.on("tensor", lambda E, kc=kc, ncn=ncn, col0=col0, buf=buf: E.matmul(
                        psum_mod[ncn][0:1, :], c_bf[:, kc:kc + 1], buf[:, col0:col0 + 512],
                        start=(kc == 0), stop=(kc == NK - 1)), inc="cT")
                wmod_war[kc % 2] = tok
            em.wait("vector", tok)
            for ncn in range(6):
                col0 = (p2 * 6 + ncn) * 512
                tv = em.on("vector", lambda E, ncn=ncn, col0=col0: E.tensor_copy(
                    mod_row[0:1, col0:col0 + 512], psum_mod[ncn][0:1, :]), inc="cV")
            ev_toks.append(tv)
            if p2 == 0:
                em.wait("tensor", tv)

        def rsl(t_, i):
            return t_[0:1, i * D:(i + 1) * D]

        sa_row, ca_row, ga_row = rsl(mod_row, 0), rsl(mod_row, 1), rsl(mod_row, 2)
        sbr_row, cb_row, gbm_row = rsl(mod_row, 3), rsl(mod_row, 4), rsl(mod_row, 5)
        ga_c, gb_c = rsl(drv, 0), rsl(drv, 1)
        t1r, t2r = rsl(tmp_row, 0), rsl(tmp_row, 1)

        em.waits("vector", ev_toks + [t_lng, t_lnb])
        for g_in, g_out in ((ga_row, ga_c), (gbm_row, gb_c)):
            em.vt(lambda E, g_in=g_in: E.tensor_scalar(t1r, g_in, 0.0, None, OP.is_ge))
            em.vt(lambda E: E.tensor_scalar(t1r, t1r, 2.0, -1.0, OP.mult, OP.add))
            em.vt(lambda E, g_in=g_in: E.tensor_scalar(t2r, g_in, -1.0, None, OP.mult))
            em.vt(lambda E, g_in=g_in: E.tensor_tensor(t2r, g_in, t2r, OP.max))
            em.vt(lambda E: E.tensor_scalar(t2r, t2r, 0.25, None, OP.max))
            em.vt(lambda E, g_out=g_out: E.tensor_tensor(g_out, t1r, t2r, OP.mult))
        g2_row, b2_row = ga_row, gbm_row
        em.vt(lambda E: E.tensor_tensor(g2_row, ga_c, lng_row[0:1, :], OP.mult))
        em.vt(lambda E: E.tensor_tensor(b2_row, ga_c, lnb_row[0:1, :], OP.mult))
        em.vt(lambda E: E.tensor_scalar(ca_row, ca_row, 1.0, None, OP.add))
        t_rows = em.vt(lambda E: E.tensor_scalar(cb_row, cb_row, 1.0, None, OP.add))

        em.wait("sync", t_rows)
        t_w1 = em.dma("sync", scr_rows[0:1, :], sa_row)
        t_w2 = em.dma("sync", scr_rows[1:2, :], ca_row)
        em.waits("sync", [t_w1, t_w2])
        t_sac = em.dma("sync", sa_col[:, :], scr_rows[0:1, :].rearrange("o (k p) -> (o p) k", p=128), slow=True)
        t_cac = em.dma("sync", ca1_col[:, :], scr_rows[1:2, :].rearrange("o (k p) -> (o p) k", p=128), slow=True)

        em.wait("tensor", t_rows)
        prev_ev = None
        for row, tile in ((g2_row, g2_b), (b2_row, b2_b), (cb_row, cb1_b), (sbr_row, sb_b),
                          (gb_c, gb_b), (ca_row, ca1_b), (sa_row, sa_b)):
            for half in range(2):
                em.wait("tensor", prev_ev)
                tmm = em.on("tensor", lambda E, row=row, half=half: E.matmul(
                    psum_bc[:, :], ones_f[0:1, :], row[0:1, half * 512:(half + 1) * 512],
                    start=True, stop=True), inc="cT")
                em.wait("vector", tmm)
                prev_ev = em.on("vector", lambda E, tile=tile, half=half: E.tensor_copy(
                    tile[:, half * 512:(half + 1) * 512], psum_bc[:, :]), inc="cV")

    em.wait("tensor", prev_ev)
    em.on("tensor", lambda E: E.matmul(psum_bc[:, 0:128], ones_b[0:1, :], ones_b[0:1, 0:128],
                                       start=True, stop=True), inc="cT")

    em.barrier()

    if debug:
        dbg_ek = nc.dram_tensor("dbg_ek", [128, D], F32, kind="ExternalOutput").ap()
        dbg_rt = nc.dram_tensor("dbg_rt", [128, D], F32, kind="ExternalOutput").ap()
        dbg_rn = nc.dram_tensor("dbg_rn", [128, 16], F32, kind="ExternalOutput").ap()
        dbg_k = nc.dram_tensor("dbg_k", [128, D], F32, kind="ExternalOutput").ap()
        dbg_sT = nc.dram_tensor("dbg_sT", [128, NK * S], BF16, kind="ExternalOutput").ap()
        dbg_kps = nc.dram_tensor("dbg_kps", [NT, 128, D], F32, kind="ExternalOutput").ap()
        dbg_w = nc.dram_tensor("dbg_w", [128, NK * D], BF16, kind="ExternalOutput").ap()
    t_src = em.dma("gpsimd", scr_src[:, :], src_d[:, :])
    t_val = em.dma("gpsimd", scr_val[:, :], val_d[:, :])
    t_vinit = em.on("vector", lambda E: E.memset(v_aug[:, :], 1.0), inc="cV")

    kY = 0

    with ExitStack() as pk:
        sT_sb = sb(pk, "sT_sb", [128, NK * S], BF16)
        w_sb = sb(pk, "w_sb", [128, NK * D], BF16)
        ek_t = sb(pk, "ek_t", [128, D], F32)
        sq_t = sb(pk, "sq_t", [128, D], F32)
        r_t = sb(pk, "r_t", [128, D], F32)
        kn_bf = [sb(pk, f"kn_bf{i}", [128, D], BF16) for i in range(2)]
        psum_kv = [ps(pk, f"pskv{i}", [128, D]) for i in range(2)]

        kn_dma_toks = []
        t_vaug_last = None
        psum_free = [None, None]
        for which in ("k", "v"):
            w_in = wk_d if which == "k" else wv_d
            scr_in = scr_src if which == "k" else scr_val
            t_stage = t_src if which == "k" else t_val
            bias_row = bk_bf if which == "k" else bv_bf
            w_rows = w_in.rearrange("(c p) n -> c p n", p=128)
            if which == "v":
                em.waits("gpsimd", [t_last_mm])
                em.waits("sync", [t_last_mm])
            t_w = [em.dma("gpsimd", w_sb[:, ic * D:(ic + 1) * D], w_rows[ic]) for ic in range(NK)]
            em.wait("sync", t_stage)
            t_T = [em.dmat(sT_sb[:, ic * S:(ic + 1) * S], scr_in[:, ic * 128:(ic + 1) * 128])
                   for ic in range(NK)]
            if debug and which == "k":
                em.waits("sync", t_w + t_T)
                em.dma("sync", dbg_sT[:, :], sT_sb[:, :])
                t_dbgw = em.dma("sync", dbg_w[:, :], w_sb[:, :])
                em.waits("gpsimd", [t_dbgw])
            em.waits("tensor", t_w + t_T + [t_ones] + t_bias)
            for st in range(NT):
                pid = st % 2
                em.wait("tensor", psum_free[pid])
                for ic in range(NK):
                    stat = sT_sb[:, ic * S + st * 128: ic * S + st * 128 + 128]
                    for oc in range(2):
                        em.on("tensor", lambda E, stat=stat, ic=ic, oc=oc, pid=pid: E.matmul(
                            psum_kv[pid][:, oc * 512:(oc + 1) * 512], stat,
                            w_sb[:, ic * D + oc * 512: ic * D + (oc + 1) * 512],
                            start=(ic == 0), stop=False))
                for oc in range(2):
                    t_mm = em.on("tensor", lambda E, oc=oc, pid=pid, bias_row=bias_row: E.matmul(
                        psum_kv[pid][:, oc * 512:(oc + 1) * 512], ones_b[0:1, :],
                        bias_row[0:1, oc * 512:(oc + 1) * 512], start=False, stop=True), inc="cT")
                t_last_mm = t_mm
                if debug and which == "k" and st == 0:
                    em.wait("vector", t_mm)
                    t_dbgk = em.on("vector", lambda E, pid=pid: E.tensor_copy(ek_t[:, :], psum_kv[pid][:, :]), inc="cV")
                    em.wait("sync", t_dbgk)
                    t_dbgk2 = em.dma("sync", dbg_k[:, :], ek_t[:, :])
                    em.wait("vector", t_dbgk2)
                    em.wait("scalar", t_dbgk2)
                if debug and which == "k":
                    em.wait("vector", t_mm)
                    t_dkp = em.on("vector", lambda E, pid=pid: E.tensor_copy(sq_t[:, :], psum_kv[pid][:, :]), inc="cV")
                    em.wait("sync", t_dkp)
                    t_dkp2 = em.dma("sync", dbg_kps[st], sq_t[:, :])
                    em.wait("vector", t_dkp2)
                    em.wait("scalar", t_dkp2)
                if which == "k":
                    em.wait("scalar", t_mm)
                    em.on("scalar", lambda E, pid=pid: E.activation(r_t[:, :], psum_kv[pid][:, :], AF.Relu, scale=-1.0))
                    t_exp = em.on("scalar", lambda E: E.activation(r_t[:, :], r_t[:, :], AF.Exp, scale=-1.0), inc="cS")
                    em.wait("vector", t_exp)
                    t_ek = em.on("vector", lambda E, pid=pid: E.scalar_tensor_tensor(
                        ek_t[:, :], r_t[:, :], -1.0, psum_kv[pid][:, :], OP.add, OP.max), inc="cV")
                    psum_free[pid] = t_ek
                    if debug and which == "k" and st == 0:
                        em.wait("sync", t_exp)
                        em.dma("sync", dbg_rt[:, :], r_t[:, :])
                        em.wait("sync", t_ek)
                        t_dbg_ek = em.dma("sync", dbg_ek[:, :], ek_t[:, :])
                        em.wait("scalar", t_dbg_ek)
                        em.wait("vector", t_dbg_ek)
                    em.wait("scalar", t_ek)
                    t_sq = em.on("scalar", lambda E: E.activation(sq_t[:, :], ek_t[:, :], AF.Square), inc="cS")
                    em.wait("vector", t_sq)
                    t_red = em.on("vector", lambda E: E.tensor_reduce(
                        tiny[:, kY:kY + H], sq_t[:, :].rearrange("p (h e) -> p h e", h=H), AX.X, OP.add), inc="cV")
                    em.wait("vector", t_red)
                    t_nrm = em.on("vector", lambda E: E.tensor_copy(tiny2[:, 0:H], tiny[:, kY:kY + H]), inc="cV")
                    em.wait("scalar", t_nrm)
                    em.on("scalar", lambda E: E.activation(tiny2[:, 0:H], tiny2[:, 0:H], AF.Ln))
                    t_rn = em.on("scalar", lambda E: E.activation(tiny2[:, 0:H], tiny2[:, 0:H], AF.Exp, scale=-0.5), inc="cS")
                    em.wait("vector", t_rn)
                    if debug and which == "k" and st == 0:
                        em.wait("sync", t_rn)
                        t_dbg_rn = em.dma("sync", dbg_rn[:, :], tiny2[:, 0:16])
                        em.wait("vector", t_dbg_rn)
                        em.wait("scalar", t_dbg_rn)
                    buf = kn_bf[st % 2]
                    if len(kn_dma_toks) >= 2:
                        em.wait("vector", kn_dma_toks[-2])
                    t_knb = em.on("vector", lambda E, buf=buf: E.tensor_tensor(
                        buf[:, :].rearrange("p (h e) -> p h e", h=H),
                        ek_t[:, :].rearrange("p (h e) -> p h e", h=H),
                        tiny2[:, 0:H, None].broadcast_to([128, H, 64]), OP.mult), inc="cV")
                    em.wait("sync", t_knb)
                    tdma = em.dma("sync", scr_kn[st * 128:(st + 1) * 128, :], buf[:, :])
                    kn_dma_toks.append(tdma)
                else:
                    em.wait("vector", t_mm)
                    em.wait("vector", t_vinit)
                    base = st * (H * 65)
                    t_vv = em.on("vector", lambda E, base=base, pid=pid: E.tensor_copy(
                        v_aug[:, base:base + H * 65].rearrange("p (h e) -> p h e", e=65)[:, :, 0:64],
                        psum_kv[pid][:, :].rearrange("p (h e) -> p h e", e=64)), inc="cV")
                    psum_free[pid] = t_vv
                    t_vaug_last = t_vv
            if which == "k":
                em.waits("sync", kn_dma_toks)
                t_knT = [em.dmat(knT[:, c * S:(c + 1) * S], scr_kn[:, c * 128:(c + 1) * 128])
                         for c in range(NK)]

    em.barrier()

    hY = 16
    with ExitStack() as ph:
        x_buf = [sb(ph, f"xh{i}", [128, D], F32) for i in range(2)]
        ln_buf = [sb(ph, f"lnbuf{i}", [128, D], BF16) for i in range(2)]
        sqs = sb(ph, "sqs", [128, D], F32)

        ln1_toks = []
        free_x = [None, None]
        free_ln = [None, None]
        for t in range(NT):
            xb = x_buf[t % 2]
            em.wait("sync", free_x[t % 2])
            t_x = em.dma("sync", xb[:, :], x_src[t])
            em.wait("vector", t_x)
            t_redh = em.on("vector", lambda E, xb=xb: E.tensor_reduce(tiny[:, hY:hY + 1], xb[:, :], AX.X, OP.add), inc="cV")
            em.wait("vector", t_redh)
            t_s1 = em.vt(lambda E: E.tensor_scalar(
                tiny[:, hY + 1:hY + 2], tiny[:, hY:hY + 1], 1.0 / D, None, OP.mult))
            em.wait("scalar", t_x)
            t_ssq = em.on("scalar", lambda E, xb=xb: E.activation(
                sqs[:, :], xb[:, :], AF.Square, accum_out=tiny[:, hY + 2:hY + 3]), inc="cS")
            em.waits("vector", [t_ssq, t_s1])
            em.vt(lambda E: E.tensor_scalar(tiny[:, hY + 3:hY + 4], tiny[:, hY + 2:hY + 3], 1.0 / D, None, OP.mult))
            em.vt(lambda E: E.tensor_tensor(tiny[:, hY + 4:hY + 5], tiny[:, hY + 1:hY + 2], tiny[:, hY + 1:hY + 2], OP.mult))
            em.vt(lambda E: E.tensor_tensor(tiny[:, hY + 5:hY + 6], tiny[:, hY + 3:hY + 4], tiny[:, hY + 4:hY + 5], OP.subtract))
            t_var = em.vt(lambda E: E.tensor_scalar(tiny[:, hY + 5:hY + 6], tiny[:, hY + 5:hY + 6], LN_EPS1, None, OP.add))
            em.wait("scalar", t_var)
            em.on("scalar", lambda E: E.activation(tiny[:, hY + 6:hY + 7], tiny[:, hY + 5:hY + 6], AF.Ln))
            t_rstd = em.on("scalar", lambda E, t=t: E.activation(
                al1c[:, t:t + 1], tiny[:, hY + 6:hY + 7], AF.Exp, scale=-0.5), inc="cS")
            em.wait("vector", t_rstd)
            em.vt(lambda E: E.tensor_scalar(tiny[:, hY + 7:hY + 8], tiny[:, hY + 1:hY + 2], -1.0, None, OP.mult))
            t_beta = em.vt(lambda E, t=t: E.tensor_tensor(
                be1c[:, t:t + 1], tiny[:, hY + 7:hY + 8], al1c[:, t:t + 1], OP.mult))
            em.wait("scalar", t_beta)
            em.wait("scalar", free_ln[t % 2])
            lb = ln_buf[t % 2]
            t_lncore = em.on("scalar", lambda E, xb=xb, lb=lb, t=t: E.activation(
                lb[:, :], xb[:, :], AF.Identity, scale=al1c[:, t:t + 1], bias=be1c[:, t:t + 1]), inc="cS")
            free_x[t % 2] = t_lncore
            em.wait("sync", t_lncore)
            t_lnw = em.dma("sync", scr_ln1[t * 128:(t + 1) * 128, :], lb[:, :])
            free_ln[t % 2] = t_lnw
            ln1_toks.append(t_lnw)

    if debug:
        dal = nc.dram_tensor("dbg_al", [128, NT], F32, kind="ExternalOutput").ap()
        dbe = nc.dram_tensor("dbg_be", [128, NT], F32, kind="ExternalOutput").ap()
        em.waits("sync", ln1_toks)
        em.dma("sync", dal[:, :], al1c[:, :])
        em.dma("sync", dbe[:, :], be1c[:, :])

    em.barrier()

    qY = 24
    with ExitStack() as pq:
        wq_sb = sb(pq, "wq_sb", [128, NK * D], BF16)
        wo_sb = sb(pq, "wo_sb", [128, NK * D], BF16)
        hT_blk = sb(pq, "hT_blk", [128, NK * QB], BF16)
        hT_raw = [sb(pq, f"hTr{i}", [128, QB], BF16) for i in range(2)]
        qnT_blk = sb(pq, "qnT_blk", [128, NK * QB], BF16)
        eq_t = sb(pq, "eq_t", [128, D], F32)
        sq2_t = sb(pq, "sq2_t", [128, D], F32)
        r2_t = sq2_t
        qn_bf = [sb(pq, f"qn_bf{i}", [128, D], BF16) for i in range(2)]
        et_bufs = [sb(pq, f"et{i}", [128, 512], BF16) for i in range(4)]
        rr_bf = [sb(pq, f"rr{i}", [1, 512], BF16) for i in range(2)]
        rrb_sb = sb(pq, "rrb_sb", [64, 512], BF16)
        x_buf2 = [sb(pq, f"xq2_{i}", [128, D], F32) for i in range(2)]
        ep0 = sb(pq, "ep0", [128, D], F32)
        ep1 = eq_t
        ep2 = sb(pq, "ep2", [128, D], F32)
        ep3i = sb(pq, "ep3i", [128, D], I32)
        xq_bf = [sb(pq, f"xqb{i}", [128, D], BF16) for i in range(2)]
        xu_buf = [sb(pq, f"xub{i}", [128, D], F32) for i in range(2)]

        psum_q = ps(pq, "psq", [128, D])
        psum_sc = [ps(pq, f"pssc{i}", [128, 512]) for i in range(2)]
        psum_ctx = [ps(pq, f"psctx{i}", [128, 512]) for i in range(2)]
        psum_bc2 = ps(pq, "psbc2", [64, 512])

        w_rq = wq_d.rearrange("(c p) n -> c p n", p=128)
        w_ro = wo_d.rearrange("(c p) n -> c p n", p=128)
        t_wq = [em.dma("gpsimd", wq_sb[:, ic * D:(ic + 1) * D], w_rq[ic]) for ic in range(NK)]
        t_wo = [em.dma("gpsimd", wo_sb[:, ic * D:(ic + 1) * D], w_ro[ic]) for ic in range(NK)]

        em.waits("tensor", t_wq + t_wo)
        em.waits("scalar", [t_sac, t_cac])

        free_psq = None
        free_psc = [None, None]
        free_pctx = [None, None]
        free_pbc = None
        free_hraw = [None, None]
        free_qnbf = [None, None]
        free_et = [None] * 4
        free_x2 = [None, None]
        free_xqb = [None, None]
        free_xu = [None, None]
        prev_oproj = None
        prev_scores = None
        prev_x2v = None
        xq_dma_toks = []
        xu_dma_toks = []

        for b in range(NBLK):
            rows0 = b * QB
            em.waits("sync", ln1_toks)
            em.wait("scalar", prev_oproj)
            hT_toks = []
            for kc in range(NK):
                em.wait("sync", free_hraw[kc % 2])
                t_tr = em.dmat(hT_raw[kc % 2][:, :], scr_ln1[rows0:rows0 + QB, kc * 128:(kc + 1) * 128])
                em.wait("scalar", t_tr)
                t_aff = em.on("scalar", lambda E, kc=kc: E.activation(
                    hT_blk[:, kc * QB:(kc + 1) * QB], hT_raw[kc % 2][:, :], AF.Identity,
                    scale=ca1_col[:, kc:kc + 1], bias=sa_col[:, kc:kc + 1]), inc="cS")
                free_hraw[kc % 2] = t_aff
                hT_toks.append(t_aff)
            em.waits("tensor", hT_toks)
            for tt_ in range(4):
                em.wait("tensor", free_psq)
                for ic in range(NK):
                    stat = hT_blk[:, ic * QB + tt_ * 128: ic * QB + tt_ * 128 + 128]
                    for oc in range(2):
                        em.on("tensor", lambda E, stat=stat, ic=ic, oc=oc: E.matmul(
                            psum_q[:, oc * 512:(oc + 1) * 512], stat,
                            wq_sb[:, ic * D + oc * 512: ic * D + (oc + 1) * 512],
                            start=(ic == 0), stop=False))
                for oc in range(2):
                    t_qmm = em.on("tensor", lambda E, oc=oc: E.matmul(
                        psum_q[:, oc * 512:(oc + 1) * 512], ones_b[0:1, :],
                        bq_bf[0:1, oc * 512:(oc + 1) * 512], start=False, stop=True), inc="cT")
                em.wait("scalar", t_qmm)
                em.on("scalar", lambda E: E.activation(r2_t[:, :], psum_q[:, :], AF.Relu, scale=-1.0))
                t_exp2 = em.on("scalar", lambda E: E.activation(r2_t[:, :], r2_t[:, :], AF.Exp, scale=-1.0), inc="cS")
                em.wait("vector", t_exp2)
                t_eq = em.on("vector", lambda E: E.scalar_tensor_tensor(
                    eq_t[:, :], r2_t[:, :], -1.0, psum_q[:, :], OP.add, OP.max), inc="cV")
                free_psq = t_eq
                em.wait("scalar", t_eq)
                t_sq2 = em.on("scalar", lambda E: E.activation(sq2_t[:, :], eq_t[:, :], AF.Square), inc="cS")
                em.wait("vector", t_sq2)
                t_redq = em.on("vector", lambda E: E.tensor_reduce(
                    tiny[:, qY:qY + H], sq2_t[:, :].rearrange("p (h e) -> p h e", h=H), AX.X, OP.add), inc="cV")
                em.wait("vector", t_redq)
                t_nq = em.on("vector", lambda E: E.tensor_copy(tiny2[:, 16:16 + H], tiny[:, qY:qY + H]), inc="cV")
                em.wait("scalar", t_nq)
                em.on("scalar", lambda E: E.activation(tiny2[:, 16:16 + H], tiny2[:, 16:16 + H], AF.Ln))
                t_rq = em.on("scalar", lambda E: E.activation(tiny2[:, 16:16 + H], tiny2[:, 16:16 + H], AF.Exp, scale=-0.5), inc="cS")
                em.wait("vector", t_rq)
                em.wait("vector", free_qnbf[tt_ % 2])
                qbuf = qn_bf[tt_ % 2]
                t_qn = em.on("vector", lambda E, qbuf=qbuf: E.tensor_tensor(
                    qbuf[:, :].rearrange("p (h e) -> p h e", h=H),
                    eq_t[:, :].rearrange("p (h e) -> p h e", h=H),
                    tiny2[:, 16:16 + H, None].broadcast_to([128, H, 64]), OP.mult), inc="cV")
                em.wait("sync", t_qn)
                rw = rows0 + tt_ * 128
                t_qnw = em.dma("sync", scr_qn[rw:rw + 128, :], qbuf[:, :])
                free_qnbf[tt_ % 2] = t_qnw
            em.wait("sync", t_qnw)
            em.wait("sync", prev_scores)
            t_qnT = [em.dmat(qnT_blk[:, kc * QB:(kc + 1) * QB], scr_qn[rows0:rows0 + QB, kc * 128:(kc + 1) * 128])
                     for kc in range(NK)]
            em.waits("tensor", t_qnT)

            last_cn = None
            for hp in range(NK):
                em.waits("tensor", [free_pctx[0], free_pctx[1]])
                for st in range(NT):
                    eb = (st % 2) * 2
                    for hh in range(2):
                        em.wait("tensor", free_psc[hh])
                        stat = knT[hh * 64:(hh + 1) * 64, hp * S + st * 128: hp * S + st * 128 + 128]
                        mov = qnT_blk[hh * 64:(hh + 1) * 64, hp * QB:(hp + 1) * QB]
                        t_sc = em.on("tensor", lambda E, stat=stat, mov=mov, hh=hh: E.matmul(
                            psum_sc[hh][:, :], stat, mov, start=True, stop=True,
                            tile_position=(hh * 64, 0)), inc="cT")
                    em.wait("scalar", t_sc)
                    for hh in range(2):
                        em.wait("scalar", free_et[eb + hh])
                        t_e = em.on("scalar", lambda E, eb=eb, hh=hh: E.activation(
                            et_bufs[eb + hh][:, :], psum_sc[hh][:, :], AF.Exp, scale=0.125), inc="cS")
                        free_psc[hh] = t_e
                    em.wait("tensor", t_e)
                    for hh in range(2):
                        stat = v_aug[:, st * (H * 65) + (2 * hp + hh) * 65: st * (H * 65) + (2 * hp + hh) * 65 + 65]
                        t_cx = em.on("tensor", lambda E, stat=stat, eb=eb, hh=hh, st=st: E.matmul(
                            psum_ctx[hh][0:65, :], stat, et_bufs[eb + hh][:, :],
                            start=(st == 0), stop=(st == NT - 1)), inc="cT")
                        free_et[eb + hh] = t_cx
                em.wait("vector", t_cx)
                em.wait("vector", prev_oproj if (b > 0 and hp == 0) else None)
                for hh in range(2):
                    def _rr(E, hh=hh):
                        with E.bass.allow_low_precision("softmax normalizer bf16"):
                            return E.reciprocal(rr_bf[hh][0:1, :], psum_ctx[hh][64:65, :])
                    t_rr = em.on("vector", _rr, inc="cV")
                    em.wait("tensor", t_rr)
                    em.wait("tensor", free_pbc)
                    t_bcm = em.on("tensor", lambda E, hh=hh: E.matmul(
                        psum_bc2[:, :], ones_b[0:1, 0:64], rr_bf[hh][0:1, :],
                        start=True, stop=True), inc="cT")
                    em.wait("vector", t_bcm)
                    em.on("vector", lambda E: E.tensor_copy(rrb_sb[:, :], psum_bc2[0:64, :]))
                    t_cn = em.on("vector", lambda E, hh=hh, hp=hp: E.tensor_tensor(
                        hT_blk[hh * 64:(hh + 1) * 64, hp * QB:(hp + 1) * QB],
                        psum_ctx[hh][0:64, :], rrb_sb[:, :], OP.mult), inc="cV")
                    free_pbc = t_cn
                    free_pctx[hh] = t_cn
                    last_cn = t_cn
            em.wait("tensor", last_cn)
            for tt_ in range(4):
                em.wait("tensor", free_psq)
                for cc in range(NK):
                    stat = hT_blk[:, cc * QB + tt_ * 128: cc * QB + tt_ * 128 + 128]
                    for oc in range(2):
                        em.on("tensor", lambda E, stat=stat, cc=cc, oc=oc: E.matmul(
                            psum_q[:, oc * 512:(oc + 1) * 512], stat,
                            wo_sb[:, cc * D + oc * 512: cc * D + (oc + 1) * 512],
                            start=(cc == 0), stop=False))
                for oc in range(2):
                    t_omm = em.on("tensor", lambda E, oc=oc: E.matmul(
                        psum_q[:, oc * 512:(oc + 1) * 512], ones_b[0:1, :],
                        bo_bf[0:1, oc * 512:(oc + 1) * 512], start=False, stop=True), inc="cT")
                prev_oproj = t_omm
                prev_scores = t_omm

                ti_g = b * 4 + tt_
                xb2 = x_buf2[tt_ % 2]
                em.wait("sync", free_x2[tt_ % 2])
                t_x2 = em.dma("sync", xb2[:, :], x_src[ti_g])
                em.wait("scalar", t_x2)
                em.wait("scalar", prev_x2v)
                t_hc = em.on("scalar", lambda E, xb2=xb2, ti_g=ti_g: E.activation(
                    ep0[:, :], xb2[:, :], AF.Identity, scale=al1c[:, ti_g:ti_g + 1], bias=be1c[:, ti_g:ti_g + 1]), inc="cS")
                em.wait("vector", t_hc)
                em.on("vector", lambda E: E.tensor_tensor(ep1[:, :], ep0[:, :], ca1_b[:, :], OP.mult))
                em.on("vector", lambda E: E.tensor_tensor(ep1[:, :], ep1[:, :], sa_b[:, :], OP.add))
                em.wait("vector", t_omm)
                t_x2v = em.on("vector", lambda E: E.scalar_tensor_tensor(
                    ep0[:, :], psum_q[:, :], 1.0, ep1[:, :], OP.mult, OP.add,
                    accum_out=tiny[:, qY:qY + 1]), inc="cV")
                free_psq = t_x2v
                prev_x2v = t_x2v
                em.wait("scalar", t_x2v)
                t_sq3 = em.on("scalar", lambda E: E.activation(
                    sq2_t[:, :], ep0[:, :], AF.Square, accum_out=tiny[:, qY + 2:qY + 3]), inc="cS")
                em.wait("vector", t_sq3)
                em.vt(lambda E: E.tensor_scalar(tiny[:, qY + 1:qY + 2], tiny[:, qY:qY + 1], 1.0 / D, None, OP.mult))
                em.vt(lambda E: E.tensor_scalar(tiny[:, qY + 3:qY + 4], tiny[:, qY + 2:qY + 3], 1.0 / D, None, OP.mult))
                em.vt(lambda E: E.tensor_tensor(tiny[:, qY + 4:qY + 5], tiny[:, qY + 1:qY + 2], tiny[:, qY + 1:qY + 2], OP.mult))
                em.vt(lambda E: E.tensor_tensor(tiny[:, qY + 5:qY + 6], tiny[:, qY + 3:qY + 4], tiny[:, qY + 4:qY + 5], OP.subtract))
                t_v2 = em.vt(lambda E: E.tensor_scalar(tiny[:, qY + 5:qY + 6], tiny[:, qY + 5:qY + 6], LN_EPS2, None, OP.add))
                em.wait("scalar", t_v2)
                em.on("scalar", lambda E: E.activation(tiny[:, qY + 6:qY + 7], tiny[:, qY + 5:qY + 6], AF.Ln))
                t_a2 = em.on("scalar", lambda E: E.activation(tiny[:, qY + 8:qY + 9], tiny[:, qY + 6:qY + 7], AF.Exp, scale=-0.5), inc="cS")
                em.wait("vector", t_a2)
                em.vt(lambda E: E.tensor_scalar(tiny[:, qY + 7:qY + 8], tiny[:, qY + 1:qY + 2], -1.0, None, OP.mult))
                t_b2 = em.vt(lambda E: E.tensor_tensor(
                    tiny[:, qY + 9:qY + 10], tiny[:, qY + 7:qY + 8], tiny[:, qY + 8:qY + 9], OP.mult))
                em.wait("scalar", t_b2)
                t_lc2 = em.on("scalar", lambda E: E.activation(
                    ep2[:, :], ep0[:, :], AF.Identity, scale=tiny[:, qY + 8:qY + 9], bias=tiny[:, qY + 9:qY + 10]), inc="cS")
                em.wait("vector", t_lc2)
                em.on("vector", lambda E: E.tensor_tensor(ep2[:, :], ep2[:, :], g2_b[:, :], OP.mult))
                em.on("vector", lambda E: E.tensor_tensor(ep2[:, :], ep2[:, :], b2_b[:, :], OP.add))
                em.wait("vector", free_xu[tt_ % 2])
                xu_b = xu_buf[tt_ % 2]
                t_xu = em.on("vector", lambda E, xb2=xb2, xu_b=xu_b: E.scalar_tensor_tensor(
                    xu_b[:, :], ep2[:, :], 1.0, xb2[:, :], OP.mult, OP.add,
                    accum_out=tiny[:, qY:qY + 1]), inc="cV")
                free_x2[tt_ % 2] = t_xu
                em.wait("sync", t_xu)
                t_xuw = em.dma("sync", scr_xu[ti_g * 128:(ti_g + 1) * 128, :], xu_b[:, :])
                xu_dma_toks.append(t_xuw)
                em.wait("scalar", t_xu)
                t_sq4 = em.on("scalar", lambda E, xu_b=xu_b: E.activation(
                    sq2_t[:, :], xu_b[:, :], AF.Square, accum_out=tiny[:, qY + 2:qY + 3]), inc="cS")
                em.wait("vector", t_sq4)
                em.vt(lambda E: E.tensor_scalar(tiny[:, qY + 1:qY + 2], tiny[:, qY:qY + 1], 1.0 / D, None, OP.mult))
                em.vt(lambda E: E.tensor_scalar(tiny[:, qY + 3:qY + 4], tiny[:, qY + 2:qY + 3], 1.0 / D, None, OP.mult))
                em.vt(lambda E: E.tensor_tensor(tiny[:, qY + 4:qY + 5], tiny[:, qY + 1:qY + 2], tiny[:, qY + 1:qY + 2], OP.mult))
                em.vt(lambda E: E.tensor_tensor(tiny[:, qY + 5:qY + 6], tiny[:, qY + 3:qY + 4], tiny[:, qY + 4:qY + 5], OP.subtract))
                t_v3 = em.vt(lambda E: E.tensor_scalar(tiny[:, qY + 5:qY + 6], tiny[:, qY + 5:qY + 6], LN_EPS1, None, OP.add))
                em.wait("scalar", t_v3)
                em.on("scalar", lambda E: E.activation(tiny[:, qY + 6:qY + 7], tiny[:, qY + 5:qY + 6], AF.Ln))
                t_a3 = em.on("scalar", lambda E: E.activation(tiny[:, qY + 8:qY + 9], tiny[:, qY + 6:qY + 7], AF.Exp, scale=-0.5), inc="cS")
                em.wait("vector", t_a3)
                em.vt(lambda E: E.tensor_scalar(tiny[:, qY + 7:qY + 8], tiny[:, qY + 1:qY + 2], -1.0, None, OP.mult))
                t_b3 = em.vt(lambda E: E.tensor_tensor(
                    tiny[:, qY + 9:qY + 10], tiny[:, qY + 7:qY + 8], tiny[:, qY + 8:qY + 9], OP.mult))
                em.wait("scalar", t_b3)
                t_lc3 = em.on("scalar", lambda E, xu_b=xu_b: E.activation(
                    ep2[:, :], xu_b[:, :], AF.Identity, scale=tiny[:, qY + 8:qY + 9], bias=tiny[:, qY + 9:qY + 10]), inc="cS")
                em.wait("vector", t_lc3)
                em.on("vector", lambda E: E.tensor_tensor(ep2[:, :], ep2[:, :], cb1_b[:, :], OP.mult))
                em.on("vector", lambda E: E.tensor_tensor(ep2[:, :], ep2[:, :], sb_b[:, :], OP.add))
                t_redm = em.on("vector", lambda E: E.tensor_reduce(tiny[:, qY:qY + 1], ep2[:, :], AX.X, OP.max,
                                                          apply_absolute_value=True), inc="cV")
                em.wait("vector", t_redm)
                em.vt(lambda E: E.tensor_scalar(tiny[:, qY:qY + 1], tiny[:, qY:qY + 1], 1e-5, None, OP.max))
                em.vt(lambda E, ti_g=ti_g: E.tensor_scalar(
                    rs1c[:, ti_g:ti_g + 1], tiny[:, qY:qY + 1], 1.0 / 127.0, None, OP.mult))
                em.vt(lambda E: E.reciprocal(tiny[:, qY + 1:qY + 2], tiny[:, qY:qY + 1]))
                em.vt(lambda E, ti_g=ti_g: E.tensor_scalar(
                    s1nc[:, ti_g:ti_g + 1], tiny[:, qY + 1:qY + 2], -127.0, None, OP.mult))
                t_zp = em.on("vector", lambda E, ti_g=ti_g: E.tensor_scalar(
                    ep2[:, :], ep2[:, :], s1nc[:, ti_g:ti_g + 1], 128.0, OP.mult, OP.add), inc="cV")
                em.wait("scalar", t_zp)
                t_cast = em.on("scalar", lambda E: E.activation(ep3i[:, :], ep2[:, :], AF.Copy), inc="cS")
                em.wait("vector", t_cast)
                em.on("vector", lambda E: E.scalar_tensor_tensor(
                    ep2[:, :], ep2[:, :], -0.5, ep3i[:, :], OP.add, OP.is_ge))
                em.wait("vector", free_xqb[tt_ % 2])
                xqb = xq_bf[tt_ % 2]
                t_xq = em.on("vector", lambda E, xqb=xqb: E.scalar_tensor_tensor(
                    xqb[:, :], ep2[:, :], -128.0, ep3i[:, :], OP.add, OP.add), inc="cV")
                em.wait("sync", t_xq)
                t_xqw = em.dma("sync", scr_xq[ti_g * 128:(ti_g + 1) * 128, :], xqb[:, :])
                free_xqb[tt_ % 2] = t_xqw
                free_xu[tt_ % 2] = t_xqw
                xq_dma_toks.append(t_xqw)

    attn.close()
    mid.close()
    em.barrier()

    wY = 40
    with ExitStack() as pw:
        wbuf = [sb(pw, f"wb{i}", [128, 4 * D], F32) for i in range(2)]
        wabs = sb(pw, "wabs", [128, 4 * D], F32)
        tern = [sb(pw, f"tern{i}", [128, 4 * D], BF16) for i in range(2)]
        asum = sb(pw, "asum", [128, 8], F32)
        psum_m = ps(pw, "psm", [1, 1])
        psum_mb = ps(pw, "psmb", [128, 1])

        tern_w_toks = {"wg": [], "wu": [], "wd": []}
        t_mc = None
        for wname, w_in, w_scr, mcol, nrow in (
                ("wg", wg_d, scr_wg, mwg_c, INNER),
                ("wu", wu_d, scr_wu, mwu_c, INNER),
                ("wd", wd_d, scr_wd, mwd_c, D)):
            ngrp = (nrow * (INNER * D // nrow // D) * D) // (128 * 4 * D)
            ngrp = (nrow * (D if w_in.shape[1] == D else INNER)) // (128 * 4 * D)
            w_flat = w_in.rearrange("a b -> (a b)").rearrange("(g p n) -> g p n", p=128, n=4 * D)
            scr_flat = w_scr.rearrange("a b -> (a b)").rearrange("(g p n) -> g p n", p=128, n=4 * D)
            t_grp = [None, None]
            last_abs = None
            for g in range(ngrp):
                wb = wbuf[g % 2]
                em.wait("sync", t_grp[g % 2])
                t_l = em.dma("sync", wb[:, :], w_flat[g])
                em.wait("scalar", t_l)
                t_abs = em.on("scalar", lambda E, wb=wb, g=g: E.activation(
                    wabs[:, :], wb[:, :], AF.Abs, accum_out=asum[:, g:g + 1]), inc="cS")
                t_grp[g % 2] = t_abs
                last_abs = t_abs
            em.wait("vector", last_abs)
            t_redw = em.on("vector", lambda E, ngrp=ngrp: E.tensor_reduce(tiny[:, wY:wY + 1], asum[:, 0:ngrp], AX.X, OP.add), inc="cV")
            em.wait("vector", t_redw)
            t_cs = em.on("vector", lambda E: E.tensor_copy(tiny2[:, 24:25], tiny[:, wY:wY + 1]), inc="cV")
            em.wait("tensor", t_cs)
            t_ms = em.on("tensor", lambda E: E.matmul(psum_m[0:1, 0:1], ones_c[:, 0:1], tiny2[:, 24:25],
                                                      start=True, stop=True), inc="cT")
            em.wait("vector", t_ms)
            t_mean = em.vt(lambda E, ngrp=ngrp: E.tensor_scalar(
                trow[0:1, 0:1], psum_m[0:1, 0:1], 1.0 / (ngrp * 128 * 4 * D), None, OP.mult))
            em.wait("tensor", t_mean)
            t_mb = em.on("tensor", lambda E: E.matmul(psum_mb[:, 0:1], ones_f[0:1, :], trow[0:1, 0:1],
                                                      start=True, stop=True), inc="cT")
            em.wait("vector", t_mb)
            t_mc = em.vt(lambda E, mcol=mcol: E.tensor_copy(mcol[:, :], psum_mb[:, :]))
            em.vt(lambda E, mcol=mcol: E.tensor_scalar(tiny[:, wY + 1:wY + 2], mcol[:, :], 0.5, None, OP.mult))
            t_thr = em.vt(lambda E, mcol=mcol: E.tensor_scalar(tiny[:, wY + 2:wY + 3], mcol[:, :], -0.5, None, OP.mult))
            t_tw = [None, None]
            for g in range(ngrp):
                wb = wbuf[g % 2]
                em.wait("sync", t_grp[g % 2])
                t_l = em.dma("sync", wb[:, :], w_flat[g])
                em.waits("vector", [t_l, t_thr, t_tw[g % 2]])
                em.on("vector", lambda E, wb=wb: E.tensor_scalar(wabs[:, :], wb[:, :], tiny[:, wY + 1:wY + 2], None, OP.is_ge))
                tb = tern[g % 2]
                t_t = em.on("vector", lambda E, wb=wb, tb=tb: E.scalar_tensor_tensor(
                    tb[:, :], wb[:, :], tiny[:, wY + 2:wY + 3], wabs[:, :], OP.is_le, OP.subtract), inc="cV")
                t_grp[g % 2] = t_t
                em.wait("sync", t_t)
                t_w = em.dma("sync", scr_flat[g], tb[:, :])
                t_tw[g % 2] = t_w
                tern_w_toks[wname].append(t_w)

    em.barrier()

    fY = 48
    with ExitStack() as pf:
        wgT = sb(pf, "wgT", [128, NK * INNER], BF16)
        wuT = sb(pf, "wuT", [128, NK * INNER], BF16)
        xqT_h = sb(pf, "xqT_h", [128, NK * 1024], BF16)
        gu = sb(pf, "gu", [128, INNER], F32)
        gti = sb(pf, "gti", [128, INNER], I32)
        guq = sb(pf, "guq", [128, INNER], BF16)
        sil = sb(pf, "sil", [128, 512], F32)
        psum_g = [ps(pf, f"psg{i}", [128, 512]) for i in range(2)]
        psum_u = [ps(pf, f"psu{i}", [128, 512]) for i in range(2)]

        em.waits("sync", tern_w_toks["wg"] + tern_w_toks["wu"])
        t_wgT = [em.dmat(wgT[:, ic * INNER:(ic + 1) * INNER], scr_wg[:, ic * 128:(ic + 1) * 128])
                 for ic in range(NK)]
        t_wuT = [em.dmat(wuT[:, ic * INNER:(ic + 1) * INNER], scr_wu[:, ic * 128:(ic + 1) * 128])
                 for ic in range(NK)]
        em.waits("tensor", t_wgT + t_wuT)
        em.waits("vector", [t_mc])

        guq_w_toks = []
        free_xqT = None
        free_guq = None
        free_psg = [None, None]
        free_psu = [None, None]
        prev_sil = None
        t_fmm = None
        for half in range(2):
            em.wait("sync", free_xqT)
            em.waits("sync", xq_dma_toks)
            rows0 = half * 1024
            t_xqT = [em.dmat(xqT_h[:, ic * 1024:(ic + 1) * 1024],
                             scr_xq[rows0:rows0 + 1024, ic * 128:(ic + 1) * 128])
                     for ic in range(NK)]
            em.waits("tensor", t_xqT)
            for t8 in range(8):
                ti_g = half * 8 + t8
                em.wait("vector", prev_sil)
                em.vt(lambda E, ti_g=ti_g: E.tensor_tensor(
                    tiny[:, fY:fY + 1], rs1c[:, ti_g:ti_g + 1], mwg_c[:, :], OP.mult))
                t_al = em.vt(lambda E, ti_g=ti_g: E.tensor_tensor(
                    tiny[:, fY + 1:fY + 2], rs1c[:, ti_g:ti_g + 1], mwu_c[:, :], OP.mult))
                em.wait("scalar", t_al)
                for oc in range(NK):
                    pg = psum_g[oc % 2]
                    pu = psum_u[oc % 2]
                    em.waits("tensor", [free_psg[oc % 2], free_psu[oc % 2]])
                    for ic in range(NK):
                        stat = xqT_h[:, ic * 1024 + t8 * 128: ic * 1024 + t8 * 128 + 128]
                        em.on("tensor", lambda E, stat=stat, ic=ic, oc=oc, pg=pg: E.matmul(
                            pg[:, :], stat, wgT[:, ic * INNER + oc * 512: ic * INNER + (oc + 1) * 512],
                            start=(ic == 0), stop=(ic == NK - 1)))
                        t_fmm = em.on("tensor", lambda E, stat=stat, ic=ic, oc=oc, pu=pu: E.matmul(
                            pu[:, :], stat, wuT[:, ic * INNER + oc * 512: ic * INNER + (oc + 1) * 512],
                            start=(ic == 0), stop=(ic == NK - 1)), inc="cT")
                    em.wait("scalar", t_fmm)
                    t_sil = em.on("scalar", lambda E, pg=pg: E.activation(
                        sil[:, :], pg[:, :], AF.Silu, scale=tiny[:, fY:fY + 1]), inc="cS")
                    free_psg[oc % 2] = t_sil
                    prev_sil = t_sil
                    em.wait("vector", t_sil)
                    t_gu = em.on("vector", lambda E, pu=pu, oc=oc: E.scalar_tensor_tensor(
                        gu[:, oc * 512:(oc + 1) * 512], pu[:, :], tiny[:, fY + 1:fY + 2], sil[:, :],
                        OP.mult, OP.mult), inc="cV")
                    free_psu[oc % 2] = t_gu
                t_redf = em.on("vector", lambda E: E.tensor_reduce(tiny[:, fY + 2:fY + 3], gu[:, :], AX.X, OP.max,
                                                          apply_absolute_value=True), inc="cV")
                em.wait("vector", t_redf)
                em.on("vector", lambda E: E.tensor_scalar(tiny[:, fY + 2:fY + 3], tiny[:, fY + 2:fY + 3], 1e-5, None, OP.max))
                em.on("vector", lambda E, ti_g=ti_g: E.tensor_scalar(
                    rs2c[:, ti_g:ti_g + 1], tiny[:, fY + 2:fY + 3], 1.0 / 127.0, None, OP.mult))
                em.on("vector", lambda E: E.reciprocal(tiny[:, fY + 3:fY + 4], tiny[:, fY + 2:fY + 3]))
                em.on("vector", lambda E: E.tensor_scalar(tiny[:, fY + 3:fY + 4], tiny[:, fY + 3:fY + 4], -127.0, None, OP.mult))
                t_zp2 = em.on("vector", lambda E: E.tensor_scalar(
                    gu[:, :], gu[:, :], tiny[:, fY + 3:fY + 4], 128.0, OP.mult, OP.add), inc="cV")
                em.wait("scalar", t_zp2)
                t_cast2 = em.on("scalar", lambda E: E.activation(gti[:, :], gu[:, :], AF.Copy), inc="cS")
                em.wait("vector", t_cast2)
                em.on("vector", lambda E: E.scalar_tensor_tensor(
                    gu[:, :], gu[:, :], -0.5, gti[:, :], OP.add, OP.is_ge))
                em.wait("vector", free_guq)
                t_gq = em.on("vector", lambda E: E.scalar_tensor_tensor(
                    guq[:, :], gu[:, :], -128.0, gti[:, :], OP.add, OP.add), inc="cV")
                em.wait("sync", t_gq)
                t_gw = em.dma("sync", scr_guq[ti_g * 128:(ti_g + 1) * 128, :], guq[:, :])
                free_guq = t_gw
                guq_w_toks.append(t_gw)
            free_xqT = t_fmm

    em.barrier()

    dY = 56
    with ExitStack() as pd:
        wdT = sb(pd, "wdT", [128, 32 * D], BF16)
        gqT = [sb(pd, f"gqT{i}", [128, 32 * 512], BF16) for i in range(2)]
        od0 = sb(pd, "od0", [128, D], F32)
        od1 = sb(pd, "od1", [128, D], F32)
        xu_r = [sb(pd, f"xur{i}", [128, D], F32) for i in range(2)]
        psum_d = [ps(pd, f"psd{i}", [128, D]) for i in range(2)]

        em.waits("sync", tern_w_toks["wd"])
        t_wdT = [em.dmat(wdT[:, jc * D:(jc + 1) * D], scr_wd[:, jc * 128:(jc + 1) * 128])
                 for jc in range(32)]
        em.waits("tensor", t_wdT)

        out_toks = []
        free_gqT = [None, None]
        free_psd = [None, None]
        free_xur = [None, None]
        prev_m1 = None
        prev_ty = None
        for blk in range(NBLK):
            rows0 = blk * QB
            em.wait("sync", free_gqT[blk % 2])
            em.waits("sync", guq_w_toks)
            gq_b = gqT[blk % 2]
            t_gqT = [em.dmat(gq_b[:, jc * 512:(jc + 1) * 512],
                             scr_guq[rows0:rows0 + QB, jc * 128:(jc + 1) * 128])
                     for jc in range(32)]
            em.waits("tensor", t_gqT)
            t_dmm = None
            for tt_ in range(4):
                ti_g = blk * 4 + tt_
                pd_id = tt_ % 2
                em.wait("tensor", free_psd[pd_id])
                for jc in range(32):
                    stat = gq_b[:, jc * 512 + tt_ * 128: jc * 512 + tt_ * 128 + 128]
                    for oc in range(2):
                        t_dmm = em.on("tensor", lambda E, stat=stat, jc=jc, oc=oc, pd_id=pd_id: E.matmul(
                            psum_d[pd_id][:, oc * 512:(oc + 1) * 512], stat,
                            wdT[:, jc * D + oc * 512: jc * D + (oc + 1) * 512],
                            start=(jc == 0), stop=(jc == 31)),
                            inc=("cT" if (jc == 31 and oc == 1) else None))
                em.wait("vector", prev_m1)
                t_ad = em.vt(lambda E, ti_g=ti_g: E.tensor_tensor(
                    tiny[:, dY:dY + 1], rs2c[:, ti_g:ti_g + 1], mwd_c[:, :], OP.mult))
                em.waits("scalar", [t_dmm, t_ad, prev_m1])
                t_od = em.on("scalar", lambda E, pd_id=pd_id: E.activation(
                    od0[:, :], psum_d[pd_id][:, :], AF.Identity, scale=tiny[:, dY:dY + 1]), inc="cS")
                free_psd[pd_id] = t_od
                em.wait("sync", free_xur[tt_ % 2])
                em.wait("sync", xu_dma_toks[ti_g])
                xur = xu_r[tt_ % 2]
                t_xur = em.dma("sync", xur[:, :], scr_xu[ti_g * 128:(ti_g + 1) * 128, :])
                em.waits("vector", [t_od, t_xur, prev_ty])
                t_m1 = em.on("vector", lambda E: E.tensor_tensor(od1[:, :], od0[:, :], gb_b[:, :], OP.mult), inc="cV")
                prev_m1 = t_m1
                t_out = em.on("vector", lambda E, xur=xur: E.tensor_tensor(od1[:, :], od1[:, :], xur[:, :], OP.add), inc="cV")
                free_xur[tt_ % 2] = t_out
                em.wait("sync", t_out)
                t_y = em.dma("sync", y_d[ti_g * 128:(ti_g + 1) * 128, :], od1[:, :])
                prev_ty = t_y
                out_toks.append(t_y)
            free_gqT[blk % 2] = t_dmm

        em.waits("sync", out_toks)

    with nc.Block() as block:
        @block.sync
        def _(E):
            for f in em.ops["sync"]:
                f(E)

        @block.gpsimd
        def _(E):
            for f in em.ops["gpsimd"]:
                f(E)

        @block.tensor
        def _(E):
            for f in em.ops["tensor"]:
                f(E)

        @block.vector
        def _(E):
            for f in em.ops["vector"]:
                f(E)

        @block.scalar
        def _(E):
            for f in em.ops["scalar"]:
                f(E)

    es.close()
    return nc


_CACHE = {}


def kernel(**inputs):
    def g(name):
        return np.ascontiguousarray(np.asarray(inputs[name], dtype=np.float32))

    x = g("L0_attn_target_embedding").reshape(16384, D)
    c = g("L0_c")
    shared = {
        "src": g("L0_attn_source_embedding"),
        "val": g("L0_attn_value_embedding"),
        "wq": g("L0_attn_Wq"), "wk": g("L0_attn_Wk"), "wv": g("L0_attn_Wv"), "wo": g("L0_attn_Wo"),
        "bq": g("L0_attn_bq"), "bk": g("L0_attn_bk"), "bv": g("L0_attn_bv"), "bo": g("L0_attn_bo"),
        "lng": g("L0_attn_ln_g"), "lnb": g("L0_attn_ln_b"),
        "wg": g("L0_ffn_w_gate"), "wu": g("L0_ffn_w_up"), "wd": g("L0_ffn_w_down"),
        "wmod": g("L0_wmod"),
    }
    if "nc" not in _CACHE:
        _CACHE["nc"] = build()
    nc = _CACHE["nc"]
    in_maps = []
    for core in range(8):
        m = dict(shared)
        m["x"] = np.ascontiguousarray(x[core * T:(core + 1) * T])
        m["cb"] = np.ascontiguousarray(c[core // 4])
        in_maps.append(m)
    res = run_bass_kernel_spmd(nc, in_maps, core_ids=list(range(8)))
    y = np.concatenate([np.asarray(res.results[i]["y"]) for i in range(8)], axis=0)
    return y.reshape(2, 8192, D).astype(np.float32)
